# version 19
# baseline (speedup 1.0000x reference)
import math
from contextlib import ExitStack

import numpy as np
import concourse.bass as bass
import concourse.mybir as mybir
from concourse.bass_utils import run_bass_kernel_spmd

F32 = mybir.dt.float32
BF16 = mybir.dt.bfloat16
I8 = mybir.dt.int8
I32 = mybir.dt.int32
AF = mybir.ActivationFunctionType
AL = mybir.AluOpType
AXX = mybir.AxisListType.X

NH = 32
NP = 16
QL = 1536
KVL = 512
NOPE = 128
ROPE = 64
QH = 192
VH = 128
H = 4096
S = 2048
TQ = 1024
S192 = 1.0 / math.sqrt(192.0)
EPS = 1e-5
DBG = False
N_PAIRS = NP
SKIP_P4 = False
NO_BOUND = False
NO_MASK = False
BLVL = 6


class Plan:

    def __init__(self):
        self.items = []
        self.cnt = {}
        self.self_sem = {}

    def n(self, sem):
        return self.cnt.get(id(sem), 0)

    def add(self, eng, fn, waits=(), inc=None, inc_k=1, no_selfwait=False):
        ws = [(s, v) for (s, v) in waits if v > 0]
        incs = []
        selfw = None
        if eng in self.self_sem and not no_selfwait:
            ss = self.self_sem[eng]
            selfw = (ss, self.n(ss))
            if inc is None:
                inc, inc_k = ss, 1
            assert inc is ss, "engine ops must inc their own engine sem"
        elif eng in self.self_sem:
            ss = self.self_sem[eng]
            if inc is None:
                inc, inc_k = ss, 1
        if inc is not None:
            self.cnt[id(inc)] = self.n(inc) + inc_k
            incs.append((inc, inc_k))
        self.items.append((eng, fn, ws, incs, selfw))
        return self.n(inc) if inc is not None else None

    def run(self, eng_name, e):
        last_w = {}
        for eng, fn, waits, incs, selfw in self.items:
            if eng != eng_name:
                continue
            if selfw is not None and selfw[1] > 0:
                e.wait_ge(selfw[0], selfw[1])
            for sem, v in waits:
                if last_w.get(id(sem), -1) < v:
                    e.wait_ge(sem, v)
                    last_w[id(sem)] = v
            ins = fn(e)
            for sem, k in incs:
                assert ins is not None
                ins.then_inc(sem, k)


def build():
    nc = bass.Bass("TRN2", target_bir_lowering=False)

    def din(name, shape, dt=F32):
        return nc.dram_tensor(name, list(shape), dt, kind="ExternalInput").ap()

    xT_k0 = din("xT_k0", (H, 1024))
    xT_k1 = din("xT_k1", (H, 1024))
    xT_q = din("xT_q", (H, 1024))
    x_own = din("x_own", (TQ, H))
    wqcT = din("wqcT", (H, QL))
    wqupT = din("wqupT", (QL, NH * QH))
    wkvcT = din("wkvcT", (H, 576))
    wkT = din("wkT", (KVL, NH * NOPE))
    wvT = din("wvT", (KVL, NH * VH))
    woT = din("woT", (VH, H))
    cbT = din("cbT", (H, H), I8)
    bqc_m = din("bqc_m", (128, 12))
    bqup_m = din("bqup_m", (128, 48))
    bkvc_a = din("bkvc_a", (128, 4))
    bkvc_b = din("bkvc_b", (64, 1))
    bk_m = din("bk_m", (128, NH))
    bv_m = din("bv_m", (128, NH))
    bo_mat = din("bo_mat", (128, H))
    scb_mat = din("scb_mat", (128, H))
    ones_c = din("ones_c", (128, 1))
    ones_r = din("ones_r", (1, 128))
    nsqrt_r = din("nsqrt_r", (1, 128))
    mask0 = din("mask0", (128, 1))
    mask1 = din("mask1", (128, 1))
    eye_f = din("eye_f", (128, 128))

    out_d = nc.dram_tensor("out", [TQ, H], F32, kind="ExternalOutput").ap()

    pl = Plan()
    G, SY, TE, SC, VE = "gpsimd", "sync", "tensor", "scalar", "vector"

    with ExitStack() as top:
        sem_g = top.enter_context(nc.semaphore("sem_g"))
        sem_s = top.enter_context(nc.semaphore("sem_s"))
        sem_t = top.enter_context(nc.semaphore("sem_t"))
        sem_c = top.enter_context(nc.semaphore("sem_c"))
        sem_v = top.enter_context(nc.semaphore("sem_v"))
        sem_o = top.enter_context(nc.semaphore("sem_o"))
        pl.self_sem[VE] = sem_v
        pl.self_sem[SC] = sem_c
        g_wkvc = top.enter_context(nc.semaphore("g_wkvc"))
        g_x0 = top.enter_context(nc.semaphore("g_x0"))
        g_x1 = top.enter_context(nc.semaphore("g_x1"))
        g_wq0 = top.enter_context(nc.semaphore("g_wq0"))
        g_wq1 = top.enter_context(nc.semaphore("g_wq1"))
        g_xq = top.enter_context(nc.semaphore("g_xq"))
        g_misc = top.enter_context(nc.semaphore("g_misc"))
        g_pf0 = top.enter_context(nc.semaphore("g_pf0"))
        g_pf1 = top.enter_context(nc.semaphore("g_pf1"))
        g_cb = [top.enter_context(nc.semaphore("g_cb%d" % i)) for i in range(3)]
        s_woT = top.enter_context(nc.semaphore("s_woT"))
        s_xz = [top.enter_context(nc.semaphore("s_xz%d" % i)) for i in range(2)]
        s_st = [top.enter_context(nc.semaphore("s_st%d" % i)) for i in range(2)]
        sem_gp = top.enter_context(nc.semaphore("sem_gp"))

        sb = lambda n, sh, dt: top.enter_context(nc.sbuf_tensor(n, list(sh), dt))
        c_bqc = sb("c_bqc", (128, 12), F32)
        c_bqup = sb("c_bqup", (128, 48), F32)
        c_bkva = sb("c_bkva", (128, 4), F32)
        c_bkvb = sb("c_bkvb", (128, 1), F32)
        c_bk = sb("c_bk", (128, 32), F32)
        c_bv = sb("c_bv", (128, 32), F32)
        c_ones_c = sb("c_ones_c", (128, 1), F32)
        c_ones_r = sb("c_ones_r", (1, 128), F32)
        c_nsq = sb("c_nsq", (1, 128), F32)
        c_eyeb = sb("c_eyeb", (128, 128), BF16)
        bvsum = sb("bvsum", (128, 1), F32)
        o_sumT = sb("o_sumT", (128, TQ), F32)
        stat = sb("stat", (128, 16), F32)

        pA = top.enter_context(nc.psum_tensor("pA", [128, 1024], F32))
        pB = top.enter_context(nc.psum_tensor("pB", [128, 1024], F32))
        pC = top.enter_context(nc.psum_tensor("pC", [128, 1024], F32))

        tapw = {}

        def tap(name, src_ap, shape, dt, waits):
            if not DBG:
                return
            dd = nc.dram_tensor("tap_" + name, list(shape), dt,
                                kind="ExternalOutput").ap()
            pl.add(SY, (lambda dd, src_ap: lambda e: e.dma_start(
                out=dd, in_=src_ap))(dd, src_ap), waits=list(waits),
                inc=sem_o, inc_k=16)
            tapw[name] = pl.n(sem_o)

        for dst, src in [(c_bqc, bqc_m), (c_bkva, bkvc_a), (c_bqup, bqup_m),
                         (c_bk, bk_m), (c_bv, bv_m),
                         (c_ones_c, ones_c), (c_ones_r, ones_r),
                         (c_nsq, nsqrt_r)]:
            pl.add(SY, (lambda dst, src: lambda e: e.dma_start(out=dst[:], in_=src))(dst, src),
                   inc=sem_s, inc_k=16)
        pl.add(SY, lambda e: e.dma_start(out=c_bkvb[0:64, :], in_=bkvc_b),
               inc=sem_s, inc_k=16)
        pl.add(G, lambda e: e.dma_start(out=c_eyeb[:], in_=eye_f), inc=g_misc, inc_k=16)
        consts_s = pl.n(sem_s)
        consts_g = pl.n(sem_g)
        pl.add(VE, lambda e: e.reduce_sum(bvsum[:], c_bv[:], axis=AXX),
               waits=[(sem_s, consts_s)], inc=sem_v)
        consts_v = pl.n(sem_v)

        with ExitStack() as scA:
            sbA = lambda n, sh, dt: scA.enter_context(nc.sbuf_tensor(n, list(sh), dt))
            q_cT = sbA("q_cT", (128, 12 * TQ), BF16)
            kvn = sbA("kvn", (128, 4 * S), BF16)
            krope2 = sbA("krope2", (128, S), BF16)
            krope2b = sbA("krope2b", (128, S), BF16)
            wqup_b = sbA("wqup_b", (128, 2 * 12 * 384), BF16)
            wk_b = sbA("wk_b", (128, 2 * 4 * 256), BF16)
            wv_b = sbA("wv_b", (128, 2 * 4 * 256), BF16)
            wqup3 = wqup_b[:].rearrange("p (b c m) -> p b c m", b=2, c=12)
            wk3 = wk_b[:].rearrange("p (b c m) -> p b c m", b=2, c=4)
            wv3 = wv_b[:].rearrange("p (b c m) -> p b c m", b=2, c=4)

            with ExitStack() as sc1:
                sb1 = lambda n, sh, dt: sc1.enter_context(nc.sbuf_tensor(n, list(sh), dt))
                xbuf = sb1("xbuf", (128, 32 * 1024), BF16)
                wkvc = sb1("wkvc", (128, 32 * 576), BF16)
                wqslab = sb1("wqslab", (128, 2 * 8 * 384), BF16)

                wkvc3 = wkvc[:].rearrange("p (c m) -> p c m", c=32)
                wkvcd = wkvcT.rearrange("(c p) m -> p c m", p=128)
                for g4 in range(4):
                    pl.add(G, (lambda g4: lambda e: e.dma_start(
                        out=wkvc3[:, g4 * 8:(g4 + 1) * 8, :],
                        in_=wkvcd[:, g4 * 8:(g4 + 1) * 8, :]))(g4), inc=g_wkvc, inc_k=16)
                w_wkvc = pl.n(g_wkvc)

                xd = [xT_k0.rearrange("(c p) m -> p c m", p=128),
                      xT_k1.rearrange("(c p) m -> p c m", p=128)]
                xb3 = xbuf[:].rearrange("p (b c m) -> p b c m", b=2, c=32)

                mm_done = {}
                ev_done = {}
                xq_done = {}
                psl = [pA[:, 0:512], pA[:, 512:1024], pB[:, 0:512],
                       pB[:, 512:1024], pC[:, 0:512]]
                for jq in range(4):
                    buf = jq % 2
                    gx = [g_x0, g_x1][buf]
                    for g2 in range(2):
                        w = [(sem_t, mm_done[jq - 2])] if jq >= 2 else []
                        pl.add(G, (lambda jq, g2, buf: lambda e: e.dma_start(
                            out=xb3[:, buf, g2 * 16:(g2 + 1) * 16, :],
                            in_=xd[jq // 2][:, g2 * 16:(g2 + 1) * 16,
                                            (jq % 2) * 512:(jq % 2 + 1) * 512]))(jq, g2, buf),
                            waits=w, inc=gx, inc_k=16)
                    xq_done[jq] = (gx, pl.n(gx))
                    for oc in range(5):
                        osz = 128 if oc < 4 else 64
                        for d in range(32):
                            w = []
                            if d == 0 and oc == 0:
                                w = [xq_done[jq], (g_wkvc, w_wkvc)]
                                if jq >= 1:
                                    w.append((sem_c, ev_done[jq - 1]))
                            pl.add(TE, (lambda oc, d, buf, osz: lambda e: e.matmul(
                                psl[oc][0:osz, :],
                                wkvc3[:, d, oc * 128:oc * 128 + osz],
                                xb3[:, buf, d, :],
                                start=(d == 0), stop=(d == 31)))(oc, d, buf, osz),
                                waits=w,
                                inc=sem_t if (d == 31 and oc == 4) else None)
                    mm_done[jq] = pl.n(sem_t)
                    for oc in range(5):
                        if oc < 4:
                            fn = (lambda oc, jq: lambda e: e.activation(
                                kvn[:, oc * S + jq * 512: oc * S + (jq + 1) * 512],
                                psl[oc], AF.Identity, bias=c_bkva[:, oc:oc + 1]))(oc, jq)
                        else:
                            fn = (lambda jq: lambda e: e.activation(
                                krope2[0:64, jq * 512:(jq + 1) * 512],
                                psl[4][0:64, :], AF.Identity, bias=c_bkvb[0:64, :]))(jq)
                        pl.add(SC, fn,
                               waits=[(sem_t, mm_done[jq]), (sem_s, consts_s)],
                               inc=sem_c if oc == 4 else None)
                    ev_done[jq] = pl.n(sem_c)
                pl.add(VE, lambda e: e.tensor_copy(krope2b[64:128, :], krope2[0:64, :]),
                       waits=[(sem_c, ev_done[3])], inc=sem_v)
                pl.add(VE, lambda e: e.memzero(krope2[64:128, :]), inc=sem_v)
                pl.add(VE, lambda e: e.memzero(krope2b[0:64, :]), inc=sem_v)
                p1_v = pl.n(sem_v)
                p1_c = pl.n(sem_c)
                tap("kvn", kvn[:], (128, 4 * S), BF16, [(sem_c, p1_c)])
                tap("krope", krope2[:], (128, S), BF16, [(sem_v, p1_v)])

                xqd = xT_q.rearrange("(c p) m -> p c m", p=128)
                xq3 = xbuf[:].rearrange("p (c m) -> p c m", c=32)
                wqd = wqcT.rearrange("(c p) m -> p c m", p=128)
                wqs3 = wqslab[:].rearrange("p (b c m) -> p b c m", b=2, c=8)
                wq_pre = {}
                for dg0 in range(2):
                    gq = [g_wq0, g_wq1][dg0]
                    pl.add(G, (lambda d8, buf: lambda e: e.dma_start(
                        out=wqs3[:, buf, :, :],
                        in_=wqd[:, d8 * 8:(d8 + 1) * 8, 0:384]))(dg0, dg0),
                        inc=gq, inc_k=16)
                    wq_pre[dg0] = (gq, pl.n(gq))
                for g4 in range(4):
                    pl.add(G, (lambda g4: lambda e: e.dma_start(
                        out=xq3[:, g4 * 8:(g4 + 1) * 8, :],
                        in_=xqd[:, g4 * 8:(g4 + 1) * 8, :]))(g4),
                        waits=[(sem_t, mm_done[2 if g4 < 2 else 3])],
                        inc=g_xq, inc_k=16)
                    if g4 == 1:
                        w_xqA = pl.n(g_xq)
                w_xq = pl.n(g_xq)

                qmm_last = {}
                qev = {}
                for ocg in range(4):
                    for d8 in range(4):
                        dg = ocg * 4 + d8
                        buf = dg % 2
                        if dg < 2:
                            wq_done_d8 = wq_pre[dg]
                        else:
                            w = []
                            key = ((ocg, (d8 - 2) * 8 + 7) if d8 >= 2 else
                                   (ocg - 1, (d8 + 2) * 8 + 7))
                            w.append((sem_t, qmm_last[key]))
                            gq = [g_wq0, g_wq1][buf]
                            pl.add(G, (lambda ocg, d8, buf: lambda e: e.dma_start(
                                out=wqs3[:, buf, :, :],
                                in_=wqd[:, d8 * 8:(d8 + 1) * 8,
                                        ocg * 384:(ocg + 1) * 384]))(ocg, d8, buf),
                                waits=w, inc=gq, inc_k=16)
                            wq_done_d8 = (gq, pl.n(gq))
                        for dd in range(8):
                            d = d8 * 8 + dd
                            for oc in range(3):
                                for jq in range(2):
                                    psq = [pA, pB, pC][oc][:, jq * 512:(jq + 1) * 512]
                                    w2 = []
                                    if oc == 0 and jq == 0 and dd == 0:
                                        w2 = [wq_done_d8]
                                        if d == 0:
                                            w2.append((g_xq, w_xqA))
                                            w2.append((sem_c, qev[ocg - 1] if ocg > 0 else p1_c))
                                        elif d == 16 and ocg == 0:
                                            w2.append((g_xq, w_xq))
                                    pl.add(TE, (lambda oc, jq, d, dd, buf, psq=psq:
                                                lambda e: e.matmul(
                                        psq,
                                        wqs3[:, buf, dd,
                                             oc * 128:(oc + 1) * 128],
                                        xq3[:, d, jq * 512:(jq + 1) * 512],
                                        start=(d == 0), stop=(d == 31)))(oc, jq, d, dd, buf),
                                        waits=w2,
                                        inc=sem_t if (oc == 2 and jq == 1) else None)
                            qmm_last[(ocg, d)] = pl.n(sem_t)
                    for oc in range(3):
                        occ = ocg * 3 + oc
                        psrc = [pA, pB, pC][oc]
                        pl.add(SC, (lambda occ, psrc: lambda e: e.activation(
                            q_cT[:, occ * TQ:(occ + 1) * TQ], psrc[:],
                            AF.Identity, bias=c_bqc[:, occ:occ + 1]))(occ, psrc),
                            waits=[(sem_t, qmm_last[(ocg, 31)])],
                            inc=sem_c if oc == 2 else None)
                    qev[ocg] = pl.n(sem_c)
                p2_c = pl.n(sem_c)
                tap("qcT", q_cT[:], (128, 12 * TQ), BF16, [(sem_c, p2_c)])

            with ExitStack() as sc3:
                sb3 = lambda n, sh, dt: sc3.enter_context(nc.sbuf_tensor(n, list(sh), dt))
                qp = sb3("qp", (128, 3 * TQ), BF16)
                kp = sb3("kp", (128, 2 * S), BF16)
                vp = sb3("vp", (128, 16 * 256), BF16)
                attnT = sb3("attnT", (128, 16 * TQ), BF16)
                P_den = sb3("P_den", (128, TQ), F32)
                P_den2 = sb3("P_den2", (128, TQ), F32)
                r_rep = sb3("r_rep", (128, TQ), F32)
                bias_c = sb3("bias_c", (128, NH), F32)
                sq_b = sb3("sq_b", (128, 3 * TQ), BF16)
                sk_b = sb3("sk_b", (128, 2 * S), BF16)
                skr_b = sb3("skr_b", (128, S), BF16)
                onesb = sb3("onesb", (128, 1), BF16)
                onesb_r = sb3("onesb_r", (1, 128), BF16)
                r_bf = sb3("r_bf", (1, TQ), BF16)
                maskb = sb3("maskb", (128, 32), BF16)
                pD = sc3.enter_context(nc.psum_tensor("pD", [128, 1024], F32))

                wqupd = wqupT.rearrange("(c p) m -> p c m", p=128)
                wkd = wkT.rearrange("(c p) m -> p c m", p=128)
                wvd = wvT.rearrange("(c p) m -> p c m", p=128)

                pl.add(G, lambda e: e.dma_start(out=onesb[:], in_=ones_c),
                       waits=[(sem_t, pl.n(sem_t)), (sem_c, p2_c)],
                       inc=g_misc, inc_k=16)
                pl.add(G, lambda e: e.dma_start(out=onesb_r[:], in_=ones_r),
                       inc=g_misc, inc_k=16)
                pl.add(G, lambda e: e.dma_start(out=maskb[:, 0:1], in_=mask0),
                       inc=g_misc, inc_k=16)
                pl.add(G, lambda e: e.dma_start(out=maskb[:, 16:17], in_=mask1),
                       inc=g_misc, inc_k=16)
                w_onesb = pl.n(g_misc)
                pl.add(SC, lambda e: e.activation(skr_b[:], krope2[:], AF.Square),
                       waits=[(sem_v, p1_v)], inc=sem_c)

                wdma_done = {}
                proj_done = {}
                exp_done = {}
                tree_done = {}
                av_done = {}
                norm_done = {}
                rrep_done = {}
                rrepmm_done = {}
                smm_done = {}
                dmm_done = {}
                recip_done = {}
                bnd_red_done = {}

                p12_t = pl.n(sem_t)

                def prefetch(p):
                    buf = p % 2
                    gp = [g_pf0, g_pf1][buf]
                    w = ([(sem_t, proj_done[p - 2])] if p >= 2 else [])
                    pl.add(G, (lambda p, buf: lambda e: e.dma_start(
                        out=wqup3[:, buf, :, :],
                        in_=wqupd[:, :, p * 384:(p + 1) * 384]))(p, buf),
                        waits=w, inc=gp, inc_k=16)
                    pl.add(G, (lambda p, buf: lambda e: e.dma_start(
                        out=wk3[:, buf, :, :],
                        in_=wkd[:, :, p * 256:(p + 1) * 256]))(p, buf),
                        inc=gp, inc_k=16)
                    pl.add(G, (lambda p, buf: lambda e: e.dma_start(
                        out=wv3[:, buf, :, :],
                        in_=wvd[:, :, p * 256:(p + 1) * 256]))(p, buf),
                        inc=gp, inc_k=16)
                    wdma_done[p] = (gp, pl.n(gp))

                dones_done = {}
                merge_done = {}
                tailA_pending = []
                tailB_pending = []

                def emit_dones(h, t, pX, first=False, last=False):
                    w = [(sem_c, exp_done[(h, t)])]
                    if first and h > 0:
                        w.append((sem_v, norm_done[h - 1]))
                    for jq in range(2):
                        pl.add(TE, (lambda t, jq, pX=pX: lambda e: e.matmul(
                            pX[0:1, jq * 512:(jq + 1) * 512], onesb[:],
                            attnT[:, t * TQ + jq * 512: t * TQ + jq * 512 + 512],
                            start=(first and jq == 0) or (first and jq == 1),
                            stop=(last)))(t, jq),
                            waits=w if jq == 0 else [],
                            inc=sem_t if jq == 1 else None)
                    dones_done[(h, t)] = pl.n(sem_t)

                def emit_tailA(h, pX):
                    w = [(sem_v, tree_done[(h, 15)])]
                    if h > 0:
                        w.append((sem_v, norm_done[h - 1]))
                    for jq in range(2):
                        pl.add(TE, (lambda jq, pX=pX: lambda e: e.matmul(
                            pX[0:1, jq * 512:(jq + 1) * 512], c_ones_c[:],
                            P_den[:, jq * 512:(jq + 1) * 512],
                            start=True, stop=False))(jq),
                            waits=w if jq == 0 else [])
                        pl.add(TE, (lambda jq, pX=pX: lambda e: e.matmul(
                            pX[0:1, jq * 512:(jq + 1) * 512], c_ones_c[:],
                            P_den2[:, jq * 512:(jq + 1) * 512],
                            start=False, stop=True))(jq),
                            inc=sem_t if jq == 1 else None)
                    dmm_done[h] = pl.n(sem_t)
                    pl.add(VE, (lambda e, pX=pX: e.reciprocal(P_den[0:1, :], pX[0:1, :])),
                           waits=[(sem_t, dmm_done[h])], inc=sem_v)
                    pl.add(VE, lambda e: e.tensor_copy(r_bf[:], P_den[0:1, :]),
                           inc=sem_v)
                    recip_done[h] = pl.n(sem_v)

                def emit_tailB(h, pOu):
                    expw = exp_done.get((h + 1, 1), pl.n(sem_c))
                    for jq in range(2):
                        pl.add(TE, (lambda jq: lambda e: e.matmul(
                            pB[:, jq * 512:(jq + 1) * 512], onesb_r[:],
                            r_bf[:, jq * 512:(jq + 1) * 512],
                            start=True, stop=True))(jq),
                            waits=([(sem_v, recip_done[h]), (sem_c, expw),
                                    (sem_v, bnd_red_done.get((h + 1) // 2, 0))]
                                   if jq == 0 else []),
                            inc=sem_t if jq == 1 else None)
                    wt_ = pl.n(sem_t)
                    pl.add(VE, (lambda e: e.tensor_copy(r_rep[:], pB[:])),
                           waits=[(sem_t, wt_)], inc=sem_v)
                    rrep_done[h] = pl.n(sem_v)
                    wa = [(sem_t, av_done[(h, 15)])]
                    if h == 0:
                        pl.add(VE, (lambda e, pOu=pOu: e.scalar_tensor_tensor(
                            o_sumT[:], pOu[:], 1.0, r_rep[:], AL.mult, AL.mult)),
                            waits=wa, inc=sem_v)
                    else:
                        pl.add(VE, (lambda e, pOu=pOu: e.scalar_tensor_tensor(
                            r_rep[:], pOu[:], 1.0, r_rep[:], AL.mult, AL.mult)),
                            waits=wa, inc=sem_v)
                        pl.add(VE, lambda e: e.tensor_add(o_sumT[:], o_sumT[:], r_rep[:]),
                               waits=[(sem_o, tapw.get("osum_h%d" % (h - 1), 0))],
                               inc=sem_v)
                    norm_done[h] = pl.n(sem_v)
                    tap("osum_h%d" % h, o_sumT[:], (128, TQ), F32,
                        [(sem_v, norm_done[h])])

                prefetch(0)
                prefetch(1)

                for p in range(N_PAIRS):
                    buf = p % 2
                    h1 = 2 * p
                    qpe = {}
                    for mt in range(3):
                        psq = [pA, pB, pA][mt]
                        for d in range(12):
                            for jq in range(2):
                                w = []
                                if mt == 0 and d == 0 and jq == 0:
                                    w = [wdma_done[p], (sem_c, p2_c)]
                                    if p > 0:
                                        w.append((sem_c, exp_done[(h1 - 1, 15)]))
                                if mt == 2 and d == 0 and jq == 0:
                                    w.append((sem_c, qpe[0]))
                                pl.add(TE, (lambda mt, d, jq, buf, psq=psq: lambda e: e.matmul(
                                    psq[:, jq * 512:(jq + 1) * 512],
                                    wqup3[:, buf, d, mt * 128:(mt + 1) * 128],
                                    q_cT[:, d * TQ + jq * 512: d * TQ + jq * 512 + 512],
                                    start=(d == 0), stop=(d == 11)))(mt, d, jq, buf),
                                    waits=w, inc=sem_t if (d == 11 and jq == 1) else None)
                        tl = pl.n(sem_t)
                        pl.add(SC, (lambda mt, p, psq=psq: lambda e: e.activation(
                            qp[:, mt * TQ:(mt + 1) * TQ], psq[:], AF.Identity,
                            bias=c_bqup[:, p * 3 + mt: p * 3 + mt + 1]))(mt, p),
                            waits=[(sem_t, tl), (sem_o, tapw.get("qp", 0))], inc=sem_c)
                        qpe[mt] = pl.n(sem_c)
                    kev = {}
                    for hj in range(2):
                        for j4 in range(4):
                            slot = (hj * 4 + j4) % 2
                            psk = pB[:, slot * 512:(slot + 1) * 512]
                            for d in range(4):
                                w = []
                                if d == 0:
                                    if hj == 0 and j4 == 0:
                                        w.append((sem_c, qpe[1]))
                                    if (hj * 4 + j4) >= 2:
                                        w.append((sem_c, kev[(hj * 4 + j4) - 2]))
                                pl.add(TE, (lambda hj, j4, d, buf, psk=psk: lambda e: e.matmul(
                                    psk, wk3[:, buf, d, hj * 128:(hj + 1) * 128],
                                    kvn[:, d * S + j4 * 512: d * S + (j4 + 1) * 512],
                                    start=(d == 0), stop=(d == 3)))(hj, j4, d, buf),
                                    waits=w, inc=sem_t if d == 3 else None)
                            tl = pl.n(sem_t)
                            pl.add(SC, (lambda hj, j4, p, psk=psk: lambda e: e.activation(
                                kp[:, hj * S + j4 * 512: hj * S + (j4 + 1) * 512],
                                psk, AF.Identity,
                                bias=c_bk[:, 2 * p + hj: 2 * p + hj + 1]))(hj, j4, p),
                                waits=[(sem_t, tl), (sem_o, tapw.get("kp", 0))], inc=sem_c)
                            kev[hj * 4 + j4] = pl.n(sem_c)
                    vev = {}
                    for t in range(16):
                        slot = t % 2
                        psv = pA[:, slot * 512: slot * 512 + 256]
                        for d in range(4):
                            w = []
                            if d == 0:
                                if t == 0:
                                    w.append((sem_c, qpe[2]))
                                if t >= 2:
                                    w.append((sem_c, vev[t - 2]))
                            pl.add(TE, (lambda t, d, buf, psv=psv: lambda e: e.matmul(
                                psv, kvn[:, d * S + t * 128: d * S + (t + 1) * 128],
                                wv3[:, buf, d, :],
                                start=(d == 0), stop=(d == 3)))(t, d, buf),
                                waits=w, inc=sem_t if d == 3 else None)
                        tl = pl.n(sem_t)
                        pl.add(SC, (lambda t, psv=psv: lambda e: e.copy(
                            vp[:, t * 256:(t + 1) * 256], psv))(t),
                            waits=[(sem_t, tl), (sem_o, tapw.get("vp", 0))], inc=sem_c)
                        vev[t] = pl.n(sem_c)
                    proj_done[p] = pl.n(sem_t)
                    vev_all = pl.n(sem_c)
                    if p == 0:
                        tap("qp", qp[:], (128, 3 * TQ), BF16, [(sem_c, qpe[2])])
                        tap("kp", kp[:], (128, 2 * S), BF16, [(sem_c, kev[7])])
                        tap("vp", vp[:], (128, 16 * 256), BF16, [(sem_c, vev_all)])
                    if p + 2 < NP:
                        prefetch(p + 2)

                    pl.add(SC, (lambda e: e.activation(sq_b[:], qp[:], AF.Square)),
                           waits=[(sem_c, qpe[2])], inc=sem_c)
                    pl.add(SC, (lambda e: e.activation(sk_b[:], kp[:], AF.Square)),
                           waits=[(sem_c, kev[7])], inc=sem_c)
                    sq_done = pl.n(sem_c)

                    bias_done_h = {0: 0, 1: 0}
                    w0 = [(sem_c, sq_done), (g_misc, w_onesb)]
                    if p == 0:
                        w0.append((sem_c, vev_all))
                    else:
                        w0.append((sem_c, exp_done.get((2 * p - 1, 15), 0)))
                        w0.append((sem_c, exp_done.get((2 * p - 1, 14), 0)))
                        w0.append((sem_v, bnd_red_done.get(p - 1, 0)))
                    first_mm = [True]
                    for hj2 in (range(2) if not NO_BOUND else []):
                        pbank = pB if hj2 == 0 else pA
                        for jq in range(2):
                            pl.add(TE, (lambda jq, hj2, pbank=pbank: lambda e: e.matmul(
                                pbank[0:1, jq * 512:(jq + 1) * 512], onesb[:],
                                sq_b[:, hj2 * TQ + jq * 512: hj2 * TQ + (jq + 1) * 512],
                                start=True, stop=False))(jq, hj2),
                                waits=(w0 if first_mm[0] else []))
                            first_mm[0] = False
                            pl.add(TE, (lambda jq, hj2, pbank=pbank: lambda e: e.matmul(
                                pbank[0:1, jq * 512:(jq + 1) * 512],
                                (onesb[:] if NO_MASK else maskb[:, hj2 * 16:hj2 * 16 + 1]),
                                sq_b[:, 2 * TQ + jq * 512: 2 * TQ + (jq + 1) * 512],
                                start=False, stop=True))(jq, hj2))
                        for half in range(2):
                            bp = 32 if half == 0 else 64
                            for j4 in range(2):
                                jj = half * 2 + j4
                                pl.add(TE, (lambda jj, j4, hj2, bp, pbank=pbank:
                                            lambda e: e.matmul(
                                    pbank[bp:bp + 1, j4 * 512:(j4 + 1) * 512], onesb[:],
                                    sk_b[:, hj2 * S + jj * 512: hj2 * S + (jj + 1) * 512],
                                    start=True, stop=False))(jj, j4, hj2, bp))
                                pl.add(TE, (lambda jj, j4, bp, pbank=pbank:
                                            lambda e: e.matmul(
                                    pbank[bp:bp + 1, j4 * 512:(j4 + 1) * 512], onesb[:],
                                    skr_b[:, jj * 512:(jj + 1) * 512],
                                    start=False, stop=True))(jj, j4, bp),
                                    inc=sem_t if (hj2 == 1 and half == 1 and j4 == 1)
                                    else None)
                    wtb = pl.n(sem_t)
                    for hj2 in (range(2) if not NO_BOUND else []):
                        hh = 2 * p + hj2
                        pbank = pB if hj2 == 0 else pA
                        b = hj2 * 4
                        pl.add(VE, (lambda b, pbank=pbank: lambda e: e.reduce_max(
                            stat[0:1, b:b + 1], pbank[0:1, 0:1024], axis=AXX))(b),
                            waits=([(sem_t, wtb)] if hj2 == 0 else []), inc=sem_v)
                        pl.add(VE, (lambda b, pbank=pbank: lambda e: e.reduce_max(
                            stat[0:1, b + 1:b + 2], pbank[32:33, 0:1024], axis=AXX))(b),
                            inc=sem_v)
                        pl.add(VE, (lambda b, pbank=pbank: lambda e: e.reduce_max(
                            stat[0:1, b + 2:b + 3], pbank[64:65, 0:1024], axis=AXX))(b),
                            inc=sem_v)
                        pl.add(VE, (lambda b: lambda e: e.tensor_max(
                            stat[0:1, b + 1:b + 2], stat[0:1, b + 1:b + 2],
                            stat[0:1, b + 2:b + 3]))(b), inc=sem_v)
                        pl.add(VE, (lambda b: lambda e: e.tensor_mul(
                            stat[0:1, b + 3:b + 4], stat[0:1, b:b + 1],
                            stat[0:1, b + 1:b + 2]))(b), inc=sem_v)
                        wv_ = pl.n(sem_v)
                        pl.add(SC, (lambda b: lambda e: e.activation(
                            stat[0:1, b:b + 1], stat[0:1, b + 3:b + 4], AF.Ln))(b),
                            waits=[(sem_v, wv_)], inc=sem_c)
                        pl.add(SC, (lambda b: lambda e: e.activation(
                            stat[0:1, b + 1:b + 2], stat[0:1, b:b + 1], AF.Exp,
                            scale=0.5))(b), inc=sem_c)
                        wc_ = pl.n(sem_c)
                        pl.add(TE, (lambda b, hj2: lambda e: e.matmul(
                            pB[:, hj2:hj2 + 1], c_nsq[:], stat[0:1, b + 1:b + 2],
                            start=True, stop=True))(b, hj2),
                            waits=[(sem_c, wc_)], inc=sem_t)
                        wt2 = pl.n(sem_t)
                        pl.add(VE, (lambda hh, hj2: lambda e: e.tensor_copy(
                            bias_c[:, hh:hh + 1], pB[:, hj2:hj2 + 1]))(hh, hj2),
                            waits=[(sem_t, wt2)], inc=sem_v)
                        bias_done_h[hj2] = pl.n(sem_v)
                        if hh in (0, 2, 5, 13, 26, 27, 28, 31):
                            tap("biasrt_h%d" % hh, bias_c[:, hh:hh + 1], (128, 1), F32,
                                [(sem_v, bias_done_h[hj2])])
                    bnd_red_done[p] = pl.n(sem_v)
                    if p == 13:
                        tap("sq13", sq_b[:], (128, 3 * TQ), BF16, [(sem_c, sq_done)])
                        tap("sk13", sk_b[:], (128, 2 * S), BF16, [(sem_c, sq_done)])
                        tap("qp13", qp[:], (128, 3 * TQ), BF16, [(sem_c, sq_done)])
                        tap("kp13", kp[:], (128, 2 * S), BF16, [(sem_c, sq_done)])

                    for hj in range(2):
                        h = 2 * p + hj
                        pOu = pC if h % 2 == 0 else pD
                        pX = pD if h % 2 == 0 else pC

                        bias_done = bias_done_h[1]

                        for t in range(16):
                            psS = pA if t % 2 == 0 else pB
                            w = []
                            if t < 2:
                                w.append((sem_v, bias_done))
                                if hj == 0:
                                    w.append((sem_c, vev_all))
                                if h > 0:
                                    w.append((sem_c, exp_done[(h - 1, 14 + t)]))
                            else:
                                w.append((sem_c, exp_done[(h, t - 2)]))
                                if t == 3 and h > 0:
                                    w.append((sem_v, rrep_done[h - 1]))
                            for jq in range(2):
                                pl.add(TE, (lambda t, jq, hj, psS=psS: lambda e: e.matmul(
                                    psS[:, jq * 512:(jq + 1) * 512],
                                    kp[:, hj * S + t * 128:hj * S + (t + 1) * 128],
                                    qp[:, hj * TQ + jq * 512: hj * TQ + jq * 512 + 512],
                                    start=True, stop=False))(t, jq, hj),
                                    waits=w if jq == 0 else [])
                                kz = krope2 if hj == 0 else krope2b
                                pl.add(TE, (lambda t, jq, kz=kz, psS=psS: lambda e: e.matmul(
                                    psS[:, jq * 512:(jq + 1) * 512],
                                    kz[:, t * 128:(t + 1) * 128],
                                    qp[:, 2 * TQ + jq * 512: 2 * TQ + jq * 512 + 512],
                                    start=False, stop=True))(t, jq),
                                    inc=sem_t if jq == 1 else None)
                            smm_done[(h, t)] = pl.n(sem_t)

                            if t == 1 and tailA_pending:
                                tailA_pending.pop()()
                            if t == 2 and tailB_pending:
                                tailB_pending.pop()()

                            if t > 0:
                                ta = t - 1
                                w = [(sem_c, exp_done[(h, ta)])]
                                if ta == 0:
                                    if h >= 1:
                                        w.append((sem_v, recip_done[h - 1]))
                                for jq in range(2):
                                    pl.add(TE, (lambda ta, jq, hj, pOu=pOu: lambda e: e.matmul(
                                        pOu[:, jq * 512:(jq + 1) * 512],
                                        vp[:, ta * 256 + hj * 128: ta * 256 + (hj + 1) * 128],
                                        attnT[:, ta * TQ + jq * 512: ta * TQ + jq * 512 + 512],
                                        start=(ta == 0), stop=(ta == 15)))(ta, jq, hj),
                                        waits=w if jq == 0 else [],
                                        inc=sem_t if jq == 1 else None)
                                av_done[(h, ta)] = pl.n(sem_t)

                            psS = pA if t % 2 == 0 else pB
                            w = [(sem_t, smm_done[(h, t)])]
                            if h > 0:
                                w.append((sem_v, tree_done[(h - 1, t)]))
                                w.append((sem_t, av_done[(h - 1, t)]))
                            pl.add(SC, (lambda t, h, psS=psS: lambda e: e.activation(
                                attnT[:, t * TQ:(t + 1) * TQ], psS[:], AF.Exp,
                                bias=(0.0 if (NO_BOUND or BLVL < 5) else bias_c[:, h:h + 1]),
                                scale=S192))(t, h),
                                waits=w, inc=sem_c)
                            exp_done[(h, t)] = pl.n(sem_c)

                            if t >= 1:
                                tt = t - 1
                                acc = P_den[:, 0:TQ] if tt % 2 == 0 else P_den2[:]
                                w = [(sem_c, exp_done[(h, tt)])]
                                if tt <= 1 and h > 0:
                                    w.append((sem_t, dmm_done[h - 1]))
                                if tt <= 1:
                                    fn = (lambda tt, acc=acc: lambda e: e.tensor_copy(
                                        acc, attnT[:, tt * TQ:(tt + 1) * TQ]))(tt)
                                else:
                                    fn = (lambda tt, acc=acc: lambda e: e.tensor_add(
                                        acc, acc, attnT[:, tt * TQ:(tt + 1) * TQ]))(tt)
                                pl.add(VE, fn, waits=w, inc=sem_v, no_selfwait=True)
                                tree_done[(h, tt)] = pl.n(sem_v)

                        pl.add(VE, (lambda e: e.tensor_add(
                            P_den2[:], P_den2[:], attnT[:, 15 * TQ:16 * TQ])),
                            waits=[(sem_c, exp_done[(h, 15)])], inc=sem_v,
                            no_selfwait=True)
                        tree_done[(h, 15)] = pl.n(sem_v)

                        ta = 15
                        w = [(sem_c, exp_done[(h, ta)])]
                        for jq in range(2):
                            pl.add(TE, (lambda ta, jq, hj, pOu=pOu: lambda e: e.matmul(
                                pOu[:, jq * 512:(jq + 1) * 512],
                                vp[:, ta * 256 + hj * 128: ta * 256 + (hj + 1) * 128],
                                attnT[:, ta * TQ + jq * 512: ta * TQ + jq * 512 + 512],
                                start=False, stop=True))(ta, jq, hj),
                                waits=w if jq == 0 else [],
                                inc=sem_t if jq == 1 else None)
                        av_done[(h, ta)] = pl.n(sem_t)

                        tailA_pending.append(
                            (lambda h=h, pX=pX: emit_tailA(h, pX)))
                        tailB_pending.append(
                            (lambda h=h, pOu=pOu: emit_tailB(h, pOu)))

                if tailA_pending:
                    tailA_pending.pop()()
                if tailB_pending:
                    tailB_pending.pop()()
                tap("osum", o_sumT[:], (128, TQ), F32, [(sem_v, norm_done[2 * N_PAIRS - 1])])
                tap("biasall", bias_c[:], (128, NH), F32,
                    [(sem_v, norm_done[2 * N_PAIRS - 1])])
                pl.add(VE, lambda e: e.tensor_scalar_add(o_sumT[:], o_sumT[:], bvsum[:]),
                       waits=[(sem_v, norm_done[2 * N_PAIRS - 1]), (sem_v, consts_v),
                              (sem_o, tapw.get("osum", 0))], inc=sem_v)
                p3_c = pl.n(sem_c)
                p3_t = pl.n(sem_t)
                p3_v = pl.n(sem_v)

        if SKIP_P4:
            pl.add(SY, lambda e: e.dma_start(out=out_d[0:128, 0:TQ], in_=o_sumT[:]),
                   waits=[(sem_c, p3_c)], inc=sem_o, inc_k=16)
            pl.add(SY, lambda e: e.nop(),
                   waits=[(s_st[0], pl.n(s_st[0])), (s_st[1], pl.n(s_st[1])),
                          (sem_o, pl.n(sem_o))])
            with nc.Block() as block:
                @block.gpsimd
                def _(e):
                    pl.run(G, e)

                @block.sync
                def _(e):
                    pl.run(SY, e)

                @block.tensor
                def _(e):
                    pl.run(TE, e)

                @block.scalar
                def _(e):
                    pl.run(SC, e)

                @block.vector
                def _(e):
                    pl.run(VE, e)
            return nc

        with ExitStack() as sc4:
            sb4 = lambda n, sh, dt: sc4.enter_context(nc.sbuf_tensor(n, list(sh), dt))
            Zs = sb4("Zs", (128, 4 * H), F32)
            Bi = sb4("Bi", (128, H // 2), I32)
            eq = sb4("eq", (128, H // 2), F32)
            xqneg = sb4("xqneg", (128, H), BF16)
            xqT = sb4("xqT", (128, 4 * 32 * 128), BF16)
            cbb = sb4("cbb", (128, 3 * 8 * 512), BF16)
            woT_f = sb4("woT_f", (128, H), F32)
            bo_rep = sb4("bo_rep", (128, H), BF16)
            scb_rep = sb4("scb_rep", (128, H), F32)
            st4 = sb4("st4", (128, 128), F32)

            pXq = sc4.enter_context(nc.psum_tensor("pXq", [128, 1024], BF16))
            pE = sc4.enter_context(nc.psum_tensor("pE", [128, 512], F32))

            cbd = cbT.rearrange("(c p) m -> p c m", p=128)
            cb4 = cbb[:].rearrange("p (b c m) -> p b c m", b=3, c=8)
            xqT4 = xqT[:].rearrange("p (s c m) -> p s c m", s=4, c=32)
            xo3 = x_own.rearrange("(g p) m -> g p m", p=128)
            out3 = out_d.rearrange("(g p) m -> g p m", p=128)

            pl.add(G, lambda e: e.dma_start(out=bo_rep[:], in_=bo_mat),
                   waits=[(sem_t, p3_t), (sem_c, p3_c), (sem_v, p3_v)],
                   inc=g_misc, inc_k=16)
            w_bo = (g_misc, pl.n(g_misc))
            pl.add(SY, lambda e: e.dma_start(out=scb_rep[:], in_=scb_mat),
                   waits=[(sem_t, p3_t), (sem_c, p3_c), (sem_v, p3_v)],
                   inc=s_woT, inc_k=16)
            w_scb = pl.n(s_woT)
            pl.add(SY, lambda e: e.dma_start(out=woT_f[:], in_=woT),
                   inc=s_woT, inc_k=16)
            w_woT = pl.n(s_woT)
            pl.add(VE, lambda e: e.tensor_scalar_mul(
                scb_rep[:], scb_rep[:], 1.0 / 127.0),
                waits=[(s_woT, w_scb)], inc=sem_v)
            rep_done = pl.n(sem_v)

            ln2ap = {}
            st_done = {}
            cbu_done = {}
            slab_done = {}
            opev = {}
            opev_last = 0
            trev_last = 0
            tr_done = {}
            ffev = {}
            ffn_pair_done = {}
            xq_v_e = {}
            chain_guard = {-1: 0}

            def zslice(e):
                return Zs[:, (e % 4) * H:(e % 4 + 1) * H]

            def slab_load(gu):
                bufu = gu % 3
                u = gu % 32
                w = ([(sem_t, cbu_done[gu - 3])] if gu >= 3 else
                     [(sem_t, p3_t), (sem_c, p3_c), (sem_v, p3_v)])
                pl.add(G, (lambda u, bufu: lambda e: e.dma_start(
                    out=cb4[:, bufu, :, :],
                    in_=cbd[:, (u % 4) * 8:(u % 4 + 1) * 8,
                            (u // 4) * 512:(u // 4 + 1) * 512]))(u, bufu),
                    waits=w, inc=g_cb[bufu], inc_k=16)
                slab_done[gu] = (g_cb[bufu], pl.n(g_cb[bufu]))

            def emit_prep(e):
                s = e % 2
                zsl = zslice(e)
                w = [st_done.get((e - 4), (s_st[s], 0)),
                     (sem_c, ln2ap.get(e - 4, 0))]
                if e < 2:
                    w += [(sem_t, p3_t), (sem_c, p3_c), (sem_v, p3_v)]
                pl.add(SY, (lambda e, zsl=zsl: lambda ee: ee.dma_start(
                    out=zsl, in_=xo3[e, :, :]))(e),
                    waits=w, inc=s_xz[s], inc_k=16)
                xl = (s_xz[s], pl.n(s_xz[s]))
                pl.add(VE, (lambda zsl=zsl: lambda ee: ee.tensor_add(
                    zsl, zsl, bo_rep[:]))(),
                    waits=[xl] + ([w_bo] if e == 0 else []), inc=sem_v)

            def emit_opmm(e):
                nonlocal opev_last
                zsl = zslice(e)
                C = (e % 4) * 32
                for j in range(8):
                    psb = pB[:, (j % 2) * 512:(j % 2 + 1) * 512]
                    w = []
                    if e == 0 and j == 0:
                        w = [(sem_c, p3_c), (s_woT, w_woT), (sem_v, rep_done)]
                    else:
                        w.append((sem_v, opev.get((e, j - 2), opev_last)))
                    pl.add(TE, (lambda e, j, psb=psb: lambda ee: ee.matmul(
                        psb, o_sumT[:, e * 128:(e + 1) * 128],
                        woT_f[:, j * 512:(j + 1) * 512],
                        start=True, stop=True))(e, j), waits=w, inc=sem_t)
                    wt_ = pl.n(sem_t)
                    pl.add(VE, (lambda j, C, zsl=zsl, psb=psb: lambda ee:
                                ee.scalar_tensor_tensor(
                        zsl[:, j * 512:(j + 1) * 512], psb, 1.0,
                        zsl[:, j * 512:(j + 1) * 512], AL.bypass, AL.add,
                        accum_out=st4[:, C + 16 + j:C + 17 + j]))(j, C),
                        waits=[(sem_t, wt_)], inc=sem_v)
                    opev[(e, j)] = pl.n(sem_v)
                    opev_last = opev[(e, j)]

            def make_chain(e):
                zsl = zslice(e)
                C = (e % 4) * 32
                items = []
                A = items.append
                A(lambda: pl.add(VE, (lambda C=C: lambda ee: ee.reduce_sum(
                    st4[:, C:C + 1], st4[:, C + 16:C + 24], axis=AXX,
                    negate=True))(), inc=sem_v))
                A(lambda: pl.add(VE, (lambda C=C: lambda ee: ee.tensor_scalar_mul(
                    st4[:, C:C + 1], st4[:, C:C + 1], 1.0 / H))(), inc=sem_v))
                def _c():
                    wst = [(sem_v, pl.n(sem_v))]
                    pl.add(SC, (lambda zsl=zsl, C=C: lambda ee: ee.activation(
                        zsl, zsl, AF.Identity, bias=st4[:, C:C + 1]))(),
                        waits=wst, inc=sem_c)
                A(_c)
                def _mc():
                    wc_ = pl.n(sem_c)
                    pl.add(VE, (lambda zsl=zsl, C=C: lambda ee: ee.reduce_max(
                        st4[:, C + 1:C + 2], zsl, axis=AXX,
                        apply_absolute_value=True))(),
                        waits=[(sem_c, wc_)], inc=sem_v)
                A(_mc)
                ssq_c = [0]
                def _ssq():
                    pl.add(SC, (lambda zsl=zsl, C=C: lambda ee: ee.activation(
                        xqneg[:], zsl, AF.Square,
                        accum_out=st4[:, C + 2:C + 3]))(),
                        waits=[(sem_t, chain_guard.get(e - 1, 0))], inc=sem_c)
                    ssq_c[0] = pl.n(sem_c)
                A(_ssq)
                def _var():
                    wcq = pl.n(sem_c)
                    pl.add(VE, (lambda C=C: lambda ee: ee.tensor_scalar(
                        st4[:, C + 3:C + 4], st4[:, C + 2:C + 3], 1.0 / H, EPS,
                        AL.mult, AL.add))(), waits=[(sem_c, wcq)], inc=sem_v)
                A(_var)
                def _ln():
                    wv_ = pl.n(sem_v)
                    pl.add(SC, (lambda C=C: lambda ee: ee.activation(
                        st4[:, C + 4:C + 5], st4[:, C + 3:C + 4], AF.Ln))(),
                        waits=[(sem_v, wv_)], inc=sem_c)
                    pl.add(SC, (lambda C=C: lambda ee: ee.activation(
                        st4[:, C + 5:C + 6], st4[:, C + 4:C + 5], AF.Exp,
                        scale=-0.5))(), inc=sem_c)
                A(_ln)
                def _sx():
                    wc_ = pl.n(sem_c)
                    pl.add(VE, (lambda C=C: lambda ee: ee.reciprocal(
                        st4[:, C + 7:C + 8], st4[:, C + 1:C + 2]))(),
                        waits=[(sem_c, wc_)], inc=sem_v)
                    pl.add(VE, (lambda C=C: lambda ee: ee.tensor_scalar_mul(
                        st4[:, C + 7:C + 8], st4[:, C + 7:C + 8], 127.0))(),
                        inc=sem_v)
                    pl.add(VE, (lambda C=C: lambda ee: ee.tensor_mul(
                        st4[:, C + 8:C + 9], st4[:, C + 1:C + 2],
                        st4[:, C + 5:C + 6]))(), inc=sem_v)
                    pl.add(VE, (lambda C=C: lambda ee: ee.tensor_scalar_mul(
                        st4[:, C + 8:C + 9], st4[:, C + 8:C + 9], 1.0 / 127.0))(),
                        inc=sem_v)
                    pl.add(VE, (lambda C=C: lambda ee: ee.tensor_scalar_mul(
                        st4[:, C + 9:C + 10], st4[:, C + 8:C + 9], -1.0))(),
                        inc=sem_v)
                A(_sx)
                A(lambda: pl.add(VE, (lambda zsl=zsl, C=C: lambda ee:
                                      ee.tensor_scalar_mul(
                    zsl, zsl, st4[:, C + 7:C + 8]))(),
                    waits=[(sem_c, ssq_c[0])], inc=sem_v))
                HH = H // 2
                for hf in range(2):
                    a2 = zsl[:, hf * HH:(hf + 1) * HH]
                    xq2 = xqneg[:, hf * HH:(hf + 1) * HH]
                    def _i32(a2=a2):
                        pl.add(SC, (lambda a2=a2: lambda ee: ee.copy(Bi[:], a2))(),
                               waits=[(sem_v, pl.n(sem_v))], inc=sem_c)
                    A(_i32)
                    def _d1(a2=a2):
                        wc_ = pl.n(sem_c)
                        pl.add(VE, (lambda a2=a2: lambda ee:
                                    ee.scalar_tensor_tensor(
                            eq[:], a2, 0.5, Bi[:], AL.subtract, AL.is_ge))(),
                            waits=[(sem_c, wc_)], inc=sem_v)
                    A(_d1)
                    A(lambda: pl.add(VE, (lambda ee: ee.tensor_add(
                        eq[:], eq[:], Bi[:])), inc=sem_v))
                    A((lambda a2=a2, xq2=xq2: lambda: pl.add(VE,
                        (lambda a2=a2, xq2=xq2: lambda ee: ee.scalar_tensor_tensor(
                            xq2, a2, 0.5, Bi[:], AL.add, AL.is_le))(), inc=sem_v))(a2, xq2))
                    A((lambda xq2=xq2: lambda: pl.add(VE,
                        (lambda xq2=xq2: lambda ee: ee.tensor_sub(
                            xq2, xq2, eq[:]))(), inc=sem_v))(xq2))
                def _fin():
                    xq_v_e[e] = pl.n(sem_v)
                A(_fin)
                return items

            def emit_transposes(e):
                nonlocal trev_last
                s = e % 4
                for hg in range(4):
                    for k in range(8):
                        hc = hg * 8 + k
                        w = []
                        if k == 0:
                            w = [(sem_v, xq_v_e[e]), (sem_c, trev_last)]
                            if hg == 0:
                                w.append((sem_t, ffn_pair_done.get(e // 2 - 2, 0)))
                        pl.add(TE, (lambda hc, k: lambda ee: ee.transpose(
                            pXq[:, k * 128:(k + 1) * 128],
                            xqneg[:, hc * 128:(hc + 1) * 128], c_eyeb[:]))(hc, k),
                            waits=w, inc=sem_t if k == 7 else None)
                    wt_ = pl.n(sem_t)
                    pl.add(SC, (lambda hg, s: lambda ee: ee.copy(
                        xqT4[:, s, hg * 8:(hg + 1) * 8, :],
                        pXq[:].rearrange("p (c m) -> p c m", c=8)))(hg, s),
                        waits=[(sem_t, wt_)], inc=sem_c)
                    trev_last = pl.n(sem_c)
                tr_done[e] = trev_last
                chain_guard[e] = pl.n(sem_t)

            def emit_sweep(q, hook=None, hook2=None):
                e0, e1 = 2 * q, 2 * q + 1
                zs = [zslice(e0), zslice(e1)]
                Cs = [(e0 % 4) * 32, (e1 % 4) * 32]
                ss = [e0 % 4, e1 % 4]
                for oj in range(8):
                    if hook is not None:
                        hook(oj)
                    if oj % 2 == 0:
                        accs = [pA[:, 0:512], pA[:, 512:1024]]
                    else:
                        accs = [pC[:, 512:1024], pE[:, 0:512]]
                    for uu in range(4):
                        gu = q * 32 + oj * 4 + uu
                        bufu = gu % 3
                        for ei in range(2):
                            for k in range(8):
                                hc = uu * 8 + k
                                w = []
                                if uu == 0 and k == 0 and ei == 0:
                                    if oj == 0:
                                        w.append((sem_c, tr_done[e1]))
                                    pk = (q, oj - 2) if oj >= 2 else (q - 1, oj + 6)
                                    if pk in ffev:
                                        w.append((sem_v, ffev[pk]))
                                    w.append(slab_done[gu])
                                elif k == 0 and ei == 0:
                                    w.append(slab_done[gu])
                                pl.add(TE, (lambda hc, k, bufu, acc=accs[ei],
                                            sN=ss[ei], st_=(uu == 0 and k == 0),
                                            sp_=(uu == 3 and k == 7):
                                            lambda ee: ee.matmul(
                                    acc, xqT4[:, sN, hc, :], cb4[:, bufu, k, :],
                                    start=st_, stop=sp_))(hc, k, bufu),
                                    waits=w,
                                    inc=sem_t if (ei == 1 and k == 7) else None)
                        cbu_done[gu] = pl.n(sem_t)
                        if gu + 3 < 128:
                            slab_load(gu + 3)
                    wt_ = pl.n(sem_t)
                    for ei in range(2):
                        C = Cs[ei]
                        zsl = zs[ei]
                        acc = accs[ei]
                        pl.add(VE, (lambda oj, C, acc=acc: lambda ee:
                                    ee.scalar_tensor_tensor(
                            pC[:, 0:512], acc, st4[:, C + 9:C + 10],
                            scb_rep[:, oj * 512:(oj + 1) * 512],
                            AL.mult, AL.mult))(oj, C),
                            waits=[(sem_t, wt_)] if ei == 0 else [], inc=sem_v)
                        pl.add(VE, (lambda oj, C, zsl=zsl: lambda ee:
                                    ee.scalar_tensor_tensor(
                            zsl[:, oj * 512:(oj + 1) * 512],
                            zsl[:, oj * 512:(oj + 1) * 512],
                            st4[:, C + 8:C + 9], pC[:, 0:512], AL.mult, AL.add,
                            accum_out=st4[:, C + 24 + oj:C + 25 + oj]))(oj, C),
                            inc=sem_v)
                    ffev[(q, oj)] = pl.n(sem_v)
                    if hook2 is not None:
                        hook2(oj)
                ffn_pair_done[q] = pl.n(sem_t)

            def emit_back(e):
                zsl = zslice(e)
                C = (e % 4) * 32
                pl.add(VE, (lambda C=C: lambda ee: ee.reduce_sum(
                    st4[:, C + 10:C + 11], st4[:, C + 24:C + 32], axis=AXX,
                    negate=True))(),
                    waits=[(sem_v, ffev[(e // 2, 7)])], inc=sem_v)
                pl.add(VE, (lambda C=C: lambda ee: ee.tensor_scalar_mul(
                    st4[:, C + 10:C + 11], st4[:, C + 10:C + 11], 1.0 / H))(),
                    inc=sem_v)
                pl.add(SC, (lambda zsl=zsl, C=C: lambda ee: ee.activation(
                    xqneg[:], zsl, AF.Square,
                    accum_out=st4[:, C + 12:C + 13]))(),
                    waits=[(sem_t, chain_guard.get(e + 3, pl.n(sem_t))),
                           (sem_v, ffev[(e // 2, 7)])], inc=sem_c)
                wc0 = pl.n(sem_c)
                pl.add(VE, (lambda C=C: lambda ee: ee.tensor_mul(
                    st4[:, C + 13:C + 14], st4[:, C + 10:C + 11],
                    st4[:, C + 10:C + 11]))(), inc=sem_v)
                pl.add(VE, (lambda C=C: lambda ee: ee.scalar_tensor_tensor(
                    st4[:, C + 13:C + 14], st4[:, C + 12:C + 13], 1.0 / H,
                    st4[:, C + 13:C + 14], AL.mult, AL.subtract))(),
                    waits=[(sem_c, wc0)], inc=sem_v)
                pl.add(VE, (lambda C=C: lambda ee: ee.tensor_scalar(
                    st4[:, C + 13:C + 14], st4[:, C + 13:C + 14], 1.0, EPS,
                    AL.mult, AL.add))(), inc=sem_v)
                wv_ = pl.n(sem_v)
                pl.add(SC, (lambda C=C: lambda ee: ee.activation(
                    st4[:, C + 14:C + 15], st4[:, C + 13:C + 14], AF.Ln))(),
                    waits=[(sem_v, wv_)], inc=sem_c)
                pl.add(SC, (lambda C=C: lambda ee: ee.activation(
                    st4[:, C + 15:C + 16], st4[:, C + 14:C + 15], AF.Exp,
                    scale=-0.5))(), inc=sem_c)
                wc_ = pl.n(sem_c)
                pl.add(VE, (lambda C=C: lambda ee: ee.tensor_mul(
                    st4[:, C + 14:C + 15], st4[:, C + 15:C + 16],
                    st4[:, C + 15:C + 16]))(), waits=[(sem_c, wc_)], inc=sem_v)
                pl.add(VE, (lambda C=C: lambda ee: ee.tensor_mul(
                    st4[:, C + 14:C + 15], st4[:, C + 14:C + 15],
                    st4[:, C + 13:C + 14]))(), inc=sem_v)
                pl.add(VE, (lambda C=C: lambda ee: ee.tensor_scalar(
                    st4[:, C + 14:C + 15], st4[:, C + 14:C + 15], -0.5, 1.5,
                    AL.mult, AL.add))(), inc=sem_v)
                pl.add(VE, (lambda C=C: lambda ee: ee.tensor_mul(
                    st4[:, C + 15:C + 16], st4[:, C + 15:C + 16],
                    st4[:, C + 14:C + 15]))(), inc=sem_v)
                pl.add(VE, (lambda C=C: lambda ee: ee.tensor_mul(
                    st4[:, C + 11:C + 12], st4[:, C + 10:C + 11],
                    st4[:, C + 15:C + 16]))(), inc=sem_v)
                wv_ = pl.n(sem_v)
                pl.add(SC, (lambda zsl=zsl, C=C: lambda ee: ee.activation(
                    zsl, zsl, AF.Identity, bias=st4[:, C + 11:C + 12],
                    scale=st4[:, C + 15:C + 16]))(),
                    waits=[(sem_v, wv_)], inc=sem_c)
                ln2ap[e] = pl.n(sem_c)
                pl.add(SY, (lambda e, zsl=zsl: lambda ee: ee.dma_start(
                    out=out3[e, :, :], in_=zsl))(e),
                    waits=[(sem_c, ln2ap[e])], inc=s_st[e % 2], inc_k=16)
                st_done[e] = (s_st[e % 2], pl.n(s_st[e % 2]))

            for gu0 in range(3):
                slab_load(gu0)
            emit_prep(0)
            emit_prep(1)
            emit_opmm(0)
            for it in make_chain(0):
                it()
            emit_transposes(0)
            emit_opmm(1)
            for it in make_chain(1):
                it()
            emit_transposes(1)

            for q in range(4):
                if q < 3:
                    chain_items = []

                    def hookA(oj, q=q, ci=chain_items):
                        e2, e3 = 2 * q + 2, 2 * q + 3
                        if oj == 1:
                            ci.extend(make_chain(e2))
                            for _ in range((len(ci) + 2) // 3):
                                if ci:
                                    ci.pop(0)()
                        elif oj == 2:
                            for _ in range((len(ci) + 1) // 2):
                                if ci:
                                    ci.pop(0)()
                        elif oj == 3:
                            while ci:
                                ci.pop(0)()
                        elif oj == 4:
                            emit_transposes(e2)
                        elif oj == 5:
                            ci.extend(make_chain(e3))
                            for _ in range((len(ci) + 2) // 3):
                                if ci:
                                    ci.pop(0)()
                        elif oj == 6:
                            for _ in range((len(ci) + 1) // 2):
                                if ci:
                                    ci.pop(0)()
                        elif oj == 7:
                            while ci:
                                ci.pop(0)()
                            emit_transposes(e3)

                    def hookB(oj, q=q):
                        e2, e3 = 2 * q + 2, 2 * q + 3
                        if oj == 0:
                            if q > 0:
                                emit_back(2 * q - 2)
                            emit_prep(e2)
                            emit_opmm(e2)
                        elif oj == 1:
                            if q > 0:
                                emit_back(2 * q - 1)
                        elif oj == 4:
                            emit_prep(e3)
                            emit_opmm(e3)
                    emit_sweep(q, hookA, hookB)
                else:
                    def hook3B(oj):
                        if oj == 0:
                            emit_back(4)
                        elif oj == 1:
                            emit_back(5)
                    emit_sweep(q, None, hook3B)
            emit_back(6)
            emit_back(7)

            pl.add(SY, lambda e: e.nop(),
                   waits=[(s_st[0], pl.n(s_st[0])), (s_st[1], pl.n(s_st[1])),
                          (sem_o, pl.n(sem_o))])

            with nc.Block() as block:
                @block.gpsimd
                def _(e):
                    pl.run(G, e)

                @block.sync
                def _(e):
                    pl.run(SY, e)

                @block.tensor
                def _(e):
                    pl.run(TE, e)

                @block.scalar
                def _(e):
                    pl.run(SC, e)

                @block.vector
                def _(e):
                    pl.run(VE, e)
            return nc


_NC_CACHE = None


def _get_nc():
    global _NC_CACHE
    if _NC_CACHE is None:
        _NC_CACHE = build()
    return _NC_CACHE


def _shards(inputs):
    f32 = np.float32
    x = np.asarray(inputs["L0_attn_hidden_size"], f32)
    Wq_c = np.asarray(inputs["L0_attn_Wq_c"], f32)
    bq_c = np.asarray(inputs["L0_attn_bq_c"], f32)
    Wq_up = np.asarray(inputs["L0_attn_Wq_up"], f32)
    bq_up = np.asarray(inputs["L0_attn_bq_up"], f32)
    Wkv_c = np.asarray(inputs["L0_attn_Wkv_c"], f32)
    bkv_c = np.asarray(inputs["L0_attn_bkv_c"], f32)
    Wkv_up = np.asarray(inputs["L0_attn_Wkv_up"], f32)
    bkv_up = np.asarray(inputs["L0_attn_bkv_up"], f32)
    Wo = np.asarray(inputs["L0_attn_Wo"], f32)
    bo = np.asarray(inputs["L0_attn_bo"], f32)
    CB = np.asarray(inputs["L0_ffn_CB"], np.int8)
    SCB = np.asarray(inputs["L0_ffn_SCB"], f32)

    perm = []
    for p in range(NP):
        h1, h2 = 2 * p, 2 * p + 1
        perm += list(range(h1 * QH, h1 * QH + NOPE))
        perm += list(range(h2 * QH, h2 * QH + NOPE))
        perm += list(range(h1 * QH + NOPE, h1 * QH + QH))
        perm += list(range(h2 * QH + NOPE, h2 * QH + QH))
    perm = np.asarray(perm)
    kidx = np.concatenate([np.arange(h * 256, h * 256 + 128) for h in range(NH)])
    vidx = np.concatenate([np.arange(h * 256 + 128, h * 256 + 256) for h in range(NH)])

    common = dict(
        wqcT=np.ascontiguousarray(Wq_c.T),
        wqupT=np.ascontiguousarray(Wq_up.T[:, perm]),
        wkvcT=np.ascontiguousarray(Wkv_c.T),
        wkT=np.ascontiguousarray(Wkv_up.T[:, kidx]),
        wvT=np.ascontiguousarray(Wkv_up.T[:, vidx]),
        woT=np.ascontiguousarray(Wo.T),
        cbT=np.ascontiguousarray(CB.T),
        bqc_m=np.ascontiguousarray(bq_c.reshape(12, 128).T),
        bqup_m=np.ascontiguousarray(bq_up[perm].reshape(48, 128).T),
        bkvc_a=np.ascontiguousarray(bkv_c[:512].reshape(4, 128).T),
        bkvc_b=np.ascontiguousarray(bkv_c[512:].reshape(1, 64).T),
        bk_m=np.ascontiguousarray(bkv_up[kidx].reshape(32, 128).T),
        bv_m=np.ascontiguousarray(bkv_up[vidx].reshape(32, 128).T),
        bo_mat=np.ascontiguousarray(np.broadcast_to(bo.reshape(1, H), (128, H))),
        scb_mat=np.ascontiguousarray(np.broadcast_to(SCB.reshape(1, H), (128, H))),
        ones_c=np.ones((128, 1), f32),
        ones_r=np.ones((1, 128), f32),
        nsqrt_r=np.full((1, 128), -1.0 / math.sqrt(192.0), f32),
        mask0=np.concatenate([np.ones((64, 1), f32), np.zeros((64, 1), f32)]),
        mask1=np.concatenate([np.zeros((64, 1), f32), np.ones((64, 1), f32)]),
        eye_f=np.eye(128, dtype=f32),
    )
    maps = []
    for c in range(8):
        b, sh = c // 2, c % 2
        xT = np.ascontiguousarray(x[b].T)
        m = dict(common)
        m["xT_k0"] = np.ascontiguousarray(xT[:, :1024])
        m["xT_k1"] = np.ascontiguousarray(xT[:, 1024:])
        m["xT_q"] = m["xT_k0"] if sh == 0 else m["xT_k1"]
        m["x_own"] = np.ascontiguousarray(x[b, sh * 1024:(sh + 1) * 1024])
        maps.append(m)
    return maps


def kernel(**inputs):
    maps = _shards(inputs)
    nc = _get_nc()
    res = run_bass_kernel_spmd(nc, maps, core_ids=list(range(8)))
    out = np.empty((4, 2048, 4096), np.float32)
    for c in range(8):
        b, sh = c // 2, c % 2
        out[b, sh * 1024:(sh + 1) * 1024] = res.results[c]["out"]
    return out


if __name__ == "__main__":
    build()
    print("built ok")


# version 20
# speedup vs baseline: 1.0678x; 1.0678x over previous
import math
from contextlib import ExitStack

import numpy as np
import concourse.bass as bass
import concourse.mybir as mybir
from concourse.bass_utils import run_bass_kernel_spmd

F32 = mybir.dt.float32
BF16 = mybir.dt.bfloat16
I8 = mybir.dt.int8
I32 = mybir.dt.int32
AF = mybir.ActivationFunctionType
AL = mybir.AluOpType
AXX = mybir.AxisListType.X

NH = 32
NP = 16
QL = 1536
KVL = 512
NOPE = 128
ROPE = 64
QH = 192
VH = 128
H = 4096
S = 2048
TQ = 1024
S192 = 1.0 / math.sqrt(192.0)
EPS = 1e-5
DBG = False
N_PAIRS = NP
SKIP_P4 = False
NO_BOUND = False
NO_MASK = False
BLVL = 6


class Plan:

    def __init__(self):
        self.items = []
        self.cnt = {}
        self.self_sem = {}

    def n(self, sem):
        return self.cnt.get(id(sem), 0)

    def add(self, eng, fn, waits=(), inc=None, inc_k=1, no_selfwait=False):
        ws = [(s, v) for (s, v) in waits if v > 0]
        incs = []
        selfw = None
        if eng in self.self_sem and not no_selfwait:
            ss = self.self_sem[eng]
            selfw = (ss, self.n(ss))
            if inc is None:
                inc, inc_k = ss, 1
            assert inc is ss, "engine ops must inc their own engine sem"
        elif eng in self.self_sem:
            ss = self.self_sem[eng]
            if inc is None:
                inc, inc_k = ss, 1
        if inc is not None:
            self.cnt[id(inc)] = self.n(inc) + inc_k
            incs.append((inc, inc_k))
        self.items.append((eng, fn, ws, incs, selfw))
        return self.n(inc) if inc is not None else None

    def run(self, eng_name, e):
        last_w = {}
        for eng, fn, waits, incs, selfw in self.items:
            if eng != eng_name:
                continue
            if selfw is not None and selfw[1] > 0:
                e.wait_ge(selfw[0], selfw[1])
            for sem, v in waits:
                if last_w.get(id(sem), -1) < v:
                    e.wait_ge(sem, v)
                    last_w[id(sem)] = v
            ins = fn(e)
            for sem, k in incs:
                assert ins is not None
                ins.then_inc(sem, k)


def build():
    nc = bass.Bass("TRN2", target_bir_lowering=False)

    def din(name, shape, dt=F32):
        return nc.dram_tensor(name, list(shape), dt, kind="ExternalInput").ap()

    xT_k0 = din("xT_k0", (H, 1024))
    xT_k1 = din("xT_k1", (H, 1024))
    xT_q = din("xT_q", (H, 1024))
    x_own = din("x_own", (TQ, H))
    wqcT = din("wqcT", (H, QL))
    wqupT = din("wqupT", (QL, NH * QH))
    wkvcT = din("wkvcT", (H, 576))
    wkT = din("wkT", (KVL, NH * NOPE))
    wvT = din("wvT", (KVL, NH * VH))
    woT = din("woT", (VH, H))
    cbT = din("cbT", (H, H), I8)
    bqc_m = din("bqc_m", (128, 12))
    bqup_m = din("bqup_m", (128, 48))
    bkvc_a = din("bkvc_a", (128, 4))
    bkvc_b = din("bkvc_b", (64, 1))
    bk_m = din("bk_m", (128, NH))
    bv_m = din("bv_m", (128, NH))
    bo_mat = din("bo_mat", (128, H))
    scb_mat = din("scb_mat", (128, H))
    ones_c = din("ones_c", (128, 1))
    ones_r = din("ones_r", (1, 128))
    nsqrt_r = din("nsqrt_r", (1, 128))
    mask0 = din("mask0", (128, 1))
    mask1 = din("mask1", (128, 1))
    eye_f = din("eye_f", (128, 128))

    out_d = nc.dram_tensor("out", [TQ, H], F32, kind="ExternalOutput").ap()

    pl = Plan()
    G, SY, TE, SC, VE = "gpsimd", "sync", "tensor", "scalar", "vector"

    with ExitStack() as top:
        sem_g = top.enter_context(nc.semaphore("sem_g"))
        sem_s = top.enter_context(nc.semaphore("sem_s"))
        sem_t = top.enter_context(nc.semaphore("sem_t"))
        sem_c = top.enter_context(nc.semaphore("sem_c"))
        sem_v = top.enter_context(nc.semaphore("sem_v"))
        sem_o = top.enter_context(nc.semaphore("sem_o"))
        pl.self_sem[VE] = sem_v
        pl.self_sem[SC] = sem_c
        g_wkvc = top.enter_context(nc.semaphore("g_wkvc"))
        g_x0 = top.enter_context(nc.semaphore("g_x0"))
        g_x1 = top.enter_context(nc.semaphore("g_x1"))
        g_wq0 = top.enter_context(nc.semaphore("g_wq0"))
        g_wq1 = top.enter_context(nc.semaphore("g_wq1"))
        g_xq = top.enter_context(nc.semaphore("g_xq"))
        g_misc = top.enter_context(nc.semaphore("g_misc"))
        g_pf0 = top.enter_context(nc.semaphore("g_pf0"))
        g_pf1 = top.enter_context(nc.semaphore("g_pf1"))
        g_cb = [top.enter_context(nc.semaphore("g_cb%d" % i)) for i in range(3)]
        s_woT = top.enter_context(nc.semaphore("s_woT"))
        s_xz = [top.enter_context(nc.semaphore("s_xz%d" % i)) for i in range(2)]
        s_st = [top.enter_context(nc.semaphore("s_st%d" % i)) for i in range(2)]
        sem_gp = top.enter_context(nc.semaphore("sem_gp"))

        sb = lambda n, sh, dt: top.enter_context(nc.sbuf_tensor(n, list(sh), dt))
        c_bqc = sb("c_bqc", (128, 12), F32)
        c_bqup = sb("c_bqup", (128, 48), F32)
        c_bkva = sb("c_bkva", (128, 4), F32)
        c_bkvb = sb("c_bkvb", (128, 1), F32)
        c_bk = sb("c_bk", (128, 32), F32)
        c_bv = sb("c_bv", (128, 32), F32)
        c_ones_c = sb("c_ones_c", (128, 1), F32)
        c_ones_r = sb("c_ones_r", (1, 128), F32)
        c_nsq = sb("c_nsq", (1, 128), F32)
        c_eyeb = sb("c_eyeb", (128, 128), BF16)
        bvsum = sb("bvsum", (128, 1), F32)
        o_sumT = sb("o_sumT", (128, TQ), F32)
        stat = sb("stat", (128, 16), F32)

        pA = top.enter_context(nc.psum_tensor("pA", [128, 1024], F32))
        pB = top.enter_context(nc.psum_tensor("pB", [128, 1024], F32))
        pC = top.enter_context(nc.psum_tensor("pC", [128, 1024], F32))

        tapw = {}

        def tap(name, src_ap, shape, dt, waits):
            if not DBG:
                return
            dd = nc.dram_tensor("tap_" + name, list(shape), dt,
                                kind="ExternalOutput").ap()
            pl.add(SY, (lambda dd, src_ap: lambda e: e.dma_start(
                out=dd, in_=src_ap))(dd, src_ap), waits=list(waits),
                inc=sem_o, inc_k=16)
            tapw[name] = pl.n(sem_o)

        for dst, src in [(c_bqc, bqc_m), (c_bkva, bkvc_a), (c_bqup, bqup_m),
                         (c_bk, bk_m), (c_bv, bv_m),
                         (c_ones_c, ones_c), (c_ones_r, ones_r),
                         (c_nsq, nsqrt_r)]:
            pl.add(SY, (lambda dst, src: lambda e: e.dma_start(out=dst[:], in_=src))(dst, src),
                   inc=sem_s, inc_k=16)
        pl.add(SY, lambda e: e.dma_start(out=c_bkvb[0:64, :], in_=bkvc_b),
               inc=sem_s, inc_k=16)
        pl.add(G, lambda e: e.dma_start(out=c_eyeb[:], in_=eye_f), inc=g_misc, inc_k=16)
        consts_s = pl.n(sem_s)
        consts_g = pl.n(sem_g)
        pl.add(VE, lambda e: e.reduce_sum(bvsum[:], c_bv[:], axis=AXX),
               waits=[(sem_s, consts_s)], inc=sem_v)
        consts_v = pl.n(sem_v)

        with ExitStack() as scA:
            sbA = lambda n, sh, dt: scA.enter_context(nc.sbuf_tensor(n, list(sh), dt))
            q_cT = sbA("q_cT", (128, 12 * TQ), BF16)
            kvn = sbA("kvn", (128, 4 * S), BF16)
            krope2 = sbA("krope2", (128, S), BF16)
            krope2b = sbA("krope2b", (128, S), BF16)
            wqup_b = sbA("wqup_b", (128, 2 * 12 * 384), BF16)
            wk_b = sbA("wk_b", (128, 2 * 4 * 256), BF16)
            wv_b = sbA("wv_b", (128, 2 * 4 * 256), BF16)
            wqup3 = wqup_b[:].rearrange("p (b c m) -> p b c m", b=2, c=12)
            wk3 = wk_b[:].rearrange("p (b c m) -> p b c m", b=2, c=4)
            wv3 = wv_b[:].rearrange("p (b c m) -> p b c m", b=2, c=4)

            with ExitStack() as sc1:
                sb1 = lambda n, sh, dt: sc1.enter_context(nc.sbuf_tensor(n, list(sh), dt))
                xbuf = sb1("xbuf", (128, 32 * 1024), BF16)
                wkvc = sb1("wkvc", (128, 32 * 576), BF16)
                wqslab = sb1("wqslab", (128, 2 * 8 * 384), BF16)

                wkvc3 = wkvc[:].rearrange("p (c m) -> p c m", c=32)
                wkvcd = wkvcT.rearrange("(c p) m -> p c m", p=128)
                for g4 in range(4):
                    pl.add(G, (lambda g4: lambda e: e.dma_start(
                        out=wkvc3[:, g4 * 8:(g4 + 1) * 8, :],
                        in_=wkvcd[:, g4 * 8:(g4 + 1) * 8, :]))(g4), inc=g_wkvc, inc_k=16)
                w_wkvc = pl.n(g_wkvc)

                xd = [xT_k0.rearrange("(c p) m -> p c m", p=128),
                      xT_k1.rearrange("(c p) m -> p c m", p=128)]
                xb3 = xbuf[:].rearrange("p (b c m) -> p b c m", b=2, c=32)

                mm_done = {}
                ev_done = {}
                xq_done = {}
                psl = [pA[:, 0:512], pA[:, 512:1024], pB[:, 0:512],
                       pB[:, 512:1024], pC[:, 0:512]]
                for jq in range(4):
                    buf = jq % 2
                    gx = [g_x0, g_x1][buf]
                    for g2 in range(2):
                        w = [(sem_t, mm_done[jq - 2])] if jq >= 2 else []
                        pl.add(G, (lambda jq, g2, buf: lambda e: e.dma_start(
                            out=xb3[:, buf, g2 * 16:(g2 + 1) * 16, :],
                            in_=xd[jq // 2][:, g2 * 16:(g2 + 1) * 16,
                                            (jq % 2) * 512:(jq % 2 + 1) * 512]))(jq, g2, buf),
                            waits=w, inc=gx, inc_k=16)
                    xq_done[jq] = (gx, pl.n(gx))
                    for oc in range(5):
                        osz = 128 if oc < 4 else 64
                        for d in range(32):
                            w = []
                            if d == 0 and oc == 0:
                                w = [xq_done[jq], (g_wkvc, w_wkvc)]
                                if jq >= 1:
                                    w.append((sem_c, ev_done[jq - 1]))
                            pl.add(TE, (lambda oc, d, buf, osz: lambda e: e.matmul(
                                psl[oc][0:osz, :],
                                wkvc3[:, d, oc * 128:oc * 128 + osz],
                                xb3[:, buf, d, :],
                                start=(d == 0), stop=(d == 31)))(oc, d, buf, osz),
                                waits=w,
                                inc=sem_t if (d == 31 and oc == 4) else None)
                    mm_done[jq] = pl.n(sem_t)
                    for oc in range(5):
                        if oc < 4:
                            fn = (lambda oc, jq: lambda e: e.activation(
                                kvn[:, oc * S + jq * 512: oc * S + (jq + 1) * 512],
                                psl[oc], AF.Identity, bias=c_bkva[:, oc:oc + 1]))(oc, jq)
                        else:
                            fn = (lambda jq: lambda e: e.activation(
                                krope2[0:64, jq * 512:(jq + 1) * 512],
                                psl[4][0:64, :], AF.Identity, bias=c_bkvb[0:64, :]))(jq)
                        pl.add(SC, fn,
                               waits=[(sem_t, mm_done[jq]), (sem_s, consts_s)],
                               inc=sem_c if oc == 4 else None)
                    ev_done[jq] = pl.n(sem_c)
                pl.add(VE, lambda e: e.tensor_copy(krope2b[64:128, :], krope2[0:64, :]),
                       waits=[(sem_c, ev_done[3])], inc=sem_v)
                pl.add(VE, lambda e: e.memzero(krope2[64:128, :]), inc=sem_v)
                pl.add(VE, lambda e: e.memzero(krope2b[0:64, :]), inc=sem_v)
                p1_v = pl.n(sem_v)
                p1_c = pl.n(sem_c)
                tap("kvn", kvn[:], (128, 4 * S), BF16, [(sem_c, p1_c)])
                tap("krope", krope2[:], (128, S), BF16, [(sem_v, p1_v)])

                xqd = xT_q.rearrange("(c p) m -> p c m", p=128)
                xq3 = xbuf[:].rearrange("p (c m) -> p c m", c=32)
                wqd = wqcT.rearrange("(c p) m -> p c m", p=128)
                wqs3 = wqslab[:].rearrange("p (b c m) -> p b c m", b=2, c=8)
                wq_pre = {}
                for dg0 in range(2):
                    gq = [g_wq0, g_wq1][dg0]
                    pl.add(G, (lambda d8, buf: lambda e: e.dma_start(
                        out=wqs3[:, buf, :, :],
                        in_=wqd[:, d8 * 8:(d8 + 1) * 8, 0:384]))(dg0, dg0),
                        inc=gq, inc_k=16)
                    wq_pre[dg0] = (gq, pl.n(gq))
                for g4 in range(4):
                    pl.add(G, (lambda g4: lambda e: e.dma_start(
                        out=xq3[:, g4 * 8:(g4 + 1) * 8, :],
                        in_=xqd[:, g4 * 8:(g4 + 1) * 8, :]))(g4),
                        waits=[(sem_t, mm_done[2 if g4 < 2 else 3])],
                        inc=g_xq, inc_k=16)
                    if g4 == 1:
                        w_xqA = pl.n(g_xq)
                w_xq = pl.n(g_xq)

                qmm_last = {}
                qev = {}
                for ocg in range(4):
                    for d8 in range(4):
                        dg = ocg * 4 + d8
                        buf = dg % 2
                        if dg < 2:
                            wq_done_d8 = wq_pre[dg]
                        else:
                            w = []
                            key = ((ocg, (d8 - 2) * 8 + 7) if d8 >= 2 else
                                   (ocg - 1, (d8 + 2) * 8 + 7))
                            w.append((sem_t, qmm_last[key]))
                            gq = [g_wq0, g_wq1][buf]
                            pl.add(G, (lambda ocg, d8, buf: lambda e: e.dma_start(
                                out=wqs3[:, buf, :, :],
                                in_=wqd[:, d8 * 8:(d8 + 1) * 8,
                                        ocg * 384:(ocg + 1) * 384]))(ocg, d8, buf),
                                waits=w, inc=gq, inc_k=16)
                            wq_done_d8 = (gq, pl.n(gq))
                        for dd in range(8):
                            d = d8 * 8 + dd
                            for oc in range(3):
                                for jq in range(2):
                                    psq = [pA, pB, pC][oc][:, jq * 512:(jq + 1) * 512]
                                    w2 = []
                                    if oc == 0 and jq == 0 and dd == 0:
                                        w2 = [wq_done_d8]
                                        if d == 0:
                                            w2.append((g_xq, w_xqA))
                                            w2.append((sem_c, qev[ocg - 1] if ocg > 0 else p1_c))
                                        elif d == 16 and ocg == 0:
                                            w2.append((g_xq, w_xq))
                                    pl.add(TE, (lambda oc, jq, d, dd, buf, psq=psq:
                                                lambda e: e.matmul(
                                        psq,
                                        wqs3[:, buf, dd,
                                             oc * 128:(oc + 1) * 128],
                                        xq3[:, d, jq * 512:(jq + 1) * 512],
                                        start=(d == 0), stop=(d == 31)))(oc, jq, d, dd, buf),
                                        waits=w2,
                                        inc=sem_t if (oc == 2 and jq == 1) else None)
                            qmm_last[(ocg, d)] = pl.n(sem_t)
                    for oc in range(3):
                        occ = ocg * 3 + oc
                        psrc = [pA, pB, pC][oc]
                        pl.add(SC, (lambda occ, psrc: lambda e: e.activation(
                            q_cT[:, occ * TQ:(occ + 1) * TQ], psrc[:],
                            AF.Identity, bias=c_bqc[:, occ:occ + 1]))(occ, psrc),
                            waits=[(sem_t, qmm_last[(ocg, 31)])],
                            inc=sem_c if oc == 2 else None)
                    qev[ocg] = pl.n(sem_c)
                p2_c = pl.n(sem_c)
                tap("qcT", q_cT[:], (128, 12 * TQ), BF16, [(sem_c, p2_c)])

            with ExitStack() as sc3:
                sb3 = lambda n, sh, dt: sc3.enter_context(nc.sbuf_tensor(n, list(sh), dt))
                qp = sb3("qp", (128, 3 * TQ), BF16)
                kp = sb3("kp", (128, 2 * S), BF16)
                vp = sb3("vp", (128, 16 * 256), BF16)
                attnT = sb3("attnT", (128, 16 * TQ), BF16)
                P_den = sb3("P_den", (128, TQ), F32)
                P_den2 = sb3("P_den2", (128, TQ), F32)
                r_rep = sb3("r_rep", (128, TQ), F32)
                bias_c = sb3("bias_c", (128, NH), F32)
                sq_b = sb3("sq_b", (128, 3 * TQ), BF16)
                sk_b = sb3("sk_b", (128, 2 * S), BF16)
                skr_b = sb3("skr_b", (128, S), BF16)
                onesb = sb3("onesb", (128, 1), BF16)
                onesb_r = sb3("onesb_r", (1, 128), BF16)
                r_bf = sb3("r_bf", (1, TQ), BF16)
                maskb = sb3("maskb", (128, 32), BF16)
                pD = sc3.enter_context(nc.psum_tensor("pD", [128, 1024], F32))

                wqupd = wqupT.rearrange("(c p) m -> p c m", p=128)
                wkd = wkT.rearrange("(c p) m -> p c m", p=128)
                wvd = wvT.rearrange("(c p) m -> p c m", p=128)

                pl.add(G, lambda e: e.dma_start(out=onesb[:], in_=ones_c),
                       waits=[(sem_t, pl.n(sem_t)), (sem_c, p2_c)],
                       inc=g_misc, inc_k=16)
                pl.add(G, lambda e: e.dma_start(out=onesb_r[:], in_=ones_r),
                       inc=g_misc, inc_k=16)
                pl.add(G, lambda e: e.dma_start(out=maskb[:, 0:1], in_=mask0),
                       inc=g_misc, inc_k=16)
                pl.add(G, lambda e: e.dma_start(out=maskb[:, 16:17], in_=mask1),
                       inc=g_misc, inc_k=16)
                w_onesb = pl.n(g_misc)
                pl.add(SC, lambda e: e.activation(skr_b[:], krope2[:], AF.Square),
                       waits=[(sem_v, p1_v)], inc=sem_c)

                wdma_done = {}
                proj_done = {}
                exp_done = {}
                tree_done = {}
                av_done = {}
                norm_done = {}
                rrep_done = {}
                rrepmm_done = {}
                smm_done = {}
                dmm_done = {}
                recip_done = {}
                bnd_red_done = {}

                p12_t = pl.n(sem_t)

                def prefetch(p):
                    buf = p % 2
                    gp = [g_pf0, g_pf1][buf]
                    w = ([(sem_t, proj_done[p - 2])] if p >= 2 else [])
                    pl.add(G, (lambda p, buf: lambda e: e.dma_start(
                        out=wqup3[:, buf, :, :],
                        in_=wqupd[:, :, p * 384:(p + 1) * 384]))(p, buf),
                        waits=w, inc=gp, inc_k=16)
                    pl.add(G, (lambda p, buf: lambda e: e.dma_start(
                        out=wk3[:, buf, :, :],
                        in_=wkd[:, :, p * 256:(p + 1) * 256]))(p, buf),
                        inc=gp, inc_k=16)
                    pl.add(G, (lambda p, buf: lambda e: e.dma_start(
                        out=wv3[:, buf, :, :],
                        in_=wvd[:, :, p * 256:(p + 1) * 256]))(p, buf),
                        inc=gp, inc_k=16)
                    wdma_done[p] = (gp, pl.n(gp))

                dones_done = {}
                merge_done = {}
                tailA_pending = []
                tailB_pending = []

                def emit_dones(h, t, pX, first=False, last=False):
                    w = [(sem_c, exp_done[(h, t)])]
                    if first and h > 0:
                        w.append((sem_v, norm_done[h - 1]))
                    for jq in range(2):
                        pl.add(TE, (lambda t, jq, pX=pX: lambda e: e.matmul(
                            pX[0:1, jq * 512:(jq + 1) * 512], onesb[:],
                            attnT[:, t * TQ + jq * 512: t * TQ + jq * 512 + 512],
                            start=(first and jq == 0) or (first and jq == 1),
                            stop=(last)))(t, jq),
                            waits=w if jq == 0 else [],
                            inc=sem_t if jq == 1 else None)
                    dones_done[(h, t)] = pl.n(sem_t)

                def emit_tailA(h, pX):
                    w = [(sem_v, tree_done[(h, 15)])]
                    if h > 0:
                        w.append((sem_v, norm_done[h - 1]))
                    for jq in range(2):
                        pl.add(TE, (lambda jq, pX=pX: lambda e: e.matmul(
                            pX[0:1, jq * 512:(jq + 1) * 512], c_ones_c[:],
                            P_den[:, jq * 512:(jq + 1) * 512],
                            start=True, stop=False))(jq),
                            waits=w if jq == 0 else [])
                        pl.add(TE, (lambda jq, pX=pX: lambda e: e.matmul(
                            pX[0:1, jq * 512:(jq + 1) * 512], c_ones_c[:],
                            P_den2[:, jq * 512:(jq + 1) * 512],
                            start=False, stop=True))(jq),
                            inc=sem_t if jq == 1 else None)
                    dmm_done[h] = pl.n(sem_t)
                    pl.add(VE, (lambda e, pX=pX: e.reciprocal(P_den[0:1, :], pX[0:1, :])),
                           waits=[(sem_t, dmm_done[h])], inc=sem_v)
                    pl.add(VE, lambda e: e.tensor_copy(r_bf[:], P_den[0:1, :]),
                           inc=sem_v)
                    recip_done[h] = pl.n(sem_v)

                def emit_tailB(h, pOu):
                    expw = exp_done.get((h + 1, 1), pl.n(sem_c))
                    for jq in range(2):
                        pl.add(TE, (lambda jq: lambda e: e.matmul(
                            pB[:, jq * 512:(jq + 1) * 512], onesb_r[:],
                            r_bf[:, jq * 512:(jq + 1) * 512],
                            start=True, stop=True))(jq),
                            waits=([(sem_v, recip_done[h]), (sem_c, expw),
                                    (sem_v, bnd_red_done.get((h + 1) // 2, 0))]
                                   if jq == 0 else []),
                            inc=sem_t if jq == 1 else None)
                    wt_ = pl.n(sem_t)
                    pl.add(VE, (lambda e: e.tensor_copy(r_rep[:], pB[:])),
                           waits=[(sem_t, wt_)], inc=sem_v)
                    rrep_done[h] = pl.n(sem_v)
                    wa = [(sem_t, av_done[(h, 15)])]
                    if h == 0:
                        pl.add(VE, (lambda e, pOu=pOu: e.scalar_tensor_tensor(
                            o_sumT[:], pOu[:], 1.0, r_rep[:], AL.mult, AL.mult)),
                            waits=wa, inc=sem_v)
                    else:
                        pl.add(VE, (lambda e, pOu=pOu: e.scalar_tensor_tensor(
                            r_rep[:], pOu[:], 1.0, r_rep[:], AL.mult, AL.mult)),
                            waits=wa, inc=sem_v)
                        pl.add(VE, lambda e: e.tensor_add(o_sumT[:], o_sumT[:], r_rep[:]),
                               waits=[(sem_o, tapw.get("osum_h%d" % (h - 1), 0))],
                               inc=sem_v)
                    norm_done[h] = pl.n(sem_v)
                    tap("osum_h%d" % h, o_sumT[:], (128, TQ), F32,
                        [(sem_v, norm_done[h])])

                prefetch(0)
                prefetch(1)

                for p in range(N_PAIRS):
                    buf = p % 2
                    h1 = 2 * p
                    qpe = {}
                    for mt in range(3):
                        psq = [pA, pB, pA][mt]
                        for d in range(12):
                            for jq in range(2):
                                w = []
                                if mt == 0 and d == 0 and jq == 0:
                                    w = [wdma_done[p], (sem_c, p2_c)]
                                    if p > 0:
                                        w.append((sem_c, exp_done[(h1 - 1, 15)]))
                                if mt == 2 and d == 0 and jq == 0:
                                    w.append((sem_c, qpe[0]))
                                pl.add(TE, (lambda mt, d, jq, buf, psq=psq: lambda e: e.matmul(
                                    psq[:, jq * 512:(jq + 1) * 512],
                                    wqup3[:, buf, d, mt * 128:(mt + 1) * 128],
                                    q_cT[:, d * TQ + jq * 512: d * TQ + jq * 512 + 512],
                                    start=(d == 0), stop=(d == 11)))(mt, d, jq, buf),
                                    waits=w, inc=sem_t if (d == 11 and jq == 1) else None)
                        tl = pl.n(sem_t)
                        pl.add(SC, (lambda mt, p, psq=psq: lambda e: e.activation(
                            qp[:, mt * TQ:(mt + 1) * TQ], psq[:], AF.Identity,
                            bias=c_bqup[:, p * 3 + mt: p * 3 + mt + 1]))(mt, p),
                            waits=[(sem_t, tl), (sem_o, tapw.get("qp", 0))], inc=sem_c)
                        qpe[mt] = pl.n(sem_c)
                    kev = {}
                    for hj in range(2):
                        for j4 in range(4):
                            slot = (hj * 4 + j4) % 2
                            psk = pB[:, slot * 512:(slot + 1) * 512]
                            for d in range(4):
                                w = []
                                if d == 0:
                                    if hj == 0 and j4 == 0:
                                        w.append((sem_c, qpe[1]))
                                    if (hj * 4 + j4) >= 2:
                                        w.append((sem_c, kev[(hj * 4 + j4) - 2]))
                                pl.add(TE, (lambda hj, j4, d, buf, psk=psk: lambda e: e.matmul(
                                    psk, wk3[:, buf, d, hj * 128:(hj + 1) * 128],
                                    kvn[:, d * S + j4 * 512: d * S + (j4 + 1) * 512],
                                    start=(d == 0), stop=(d == 3)))(hj, j4, d, buf),
                                    waits=w, inc=sem_t if d == 3 else None)
                            tl = pl.n(sem_t)
                            pl.add(SC, (lambda hj, j4, p, psk=psk: lambda e: e.activation(
                                kp[:, hj * S + j4 * 512: hj * S + (j4 + 1) * 512],
                                psk, AF.Identity,
                                bias=c_bk[:, 2 * p + hj: 2 * p + hj + 1]))(hj, j4, p),
                                waits=[(sem_t, tl), (sem_o, tapw.get("kp", 0))], inc=sem_c)
                            kev[hj * 4 + j4] = pl.n(sem_c)
                    vev = {}
                    for t in range(16):
                        slot = t % 2
                        psv = pA[:, slot * 512: slot * 512 + 256]
                        for d in range(4):
                            w = []
                            if d == 0:
                                if t == 0:
                                    w.append((sem_c, qpe[2]))
                                if t >= 2:
                                    w.append((sem_c, vev[t - 2]))
                            pl.add(TE, (lambda t, d, buf, psv=psv: lambda e: e.matmul(
                                psv, kvn[:, d * S + t * 128: d * S + (t + 1) * 128],
                                wv3[:, buf, d, :],
                                start=(d == 0), stop=(d == 3)))(t, d, buf),
                                waits=w, inc=sem_t if d == 3 else None)
                        tl = pl.n(sem_t)
                        pl.add(SC, (lambda t, psv=psv: lambda e: e.copy(
                            vp[:, t * 256:(t + 1) * 256], psv))(t),
                            waits=[(sem_t, tl), (sem_o, tapw.get("vp", 0))], inc=sem_c)
                        vev[t] = pl.n(sem_c)
                    proj_done[p] = pl.n(sem_t)
                    vev_all = pl.n(sem_c)
                    if p == 0:
                        tap("qp", qp[:], (128, 3 * TQ), BF16, [(sem_c, qpe[2])])
                        tap("kp", kp[:], (128, 2 * S), BF16, [(sem_c, kev[7])])
                        tap("vp", vp[:], (128, 16 * 256), BF16, [(sem_c, vev_all)])
                    if p + 2 < NP:
                        prefetch(p + 2)

                    pl.add(SC, (lambda e: e.activation(sq_b[:], qp[:], AF.Square)),
                           waits=[(sem_c, qpe[2])], inc=sem_c)
                    pl.add(SC, (lambda e: e.activation(sk_b[:], kp[:], AF.Square)),
                           waits=[(sem_c, kev[7])], inc=sem_c)
                    sq_done = pl.n(sem_c)

                    bias_done_h = {0: 0, 1: 0}
                    w0 = [(sem_c, sq_done), (g_misc, w_onesb)]
                    if p == 0:
                        w0.append((sem_c, vev_all))
                    else:
                        w0.append((sem_c, exp_done.get((2 * p - 1, 15), 0)))
                        w0.append((sem_c, exp_done.get((2 * p - 1, 14), 0)))
                        w0.append((sem_v, bnd_red_done.get(p - 1, 0)))
                    first_mm = [True]
                    for hj2 in (range(2) if not NO_BOUND else []):
                        pbank = pB if hj2 == 0 else pA
                        for jq in range(2):
                            pl.add(TE, (lambda jq, hj2, pbank=pbank: lambda e: e.matmul(
                                pbank[0:1, jq * 512:(jq + 1) * 512], onesb[:],
                                sq_b[:, hj2 * TQ + jq * 512: hj2 * TQ + (jq + 1) * 512],
                                start=True, stop=False))(jq, hj2),
                                waits=(w0 if first_mm[0] else []))
                            first_mm[0] = False
                            pl.add(TE, (lambda jq, hj2, pbank=pbank: lambda e: e.matmul(
                                pbank[0:1, jq * 512:(jq + 1) * 512],
                                (onesb[:] if NO_MASK else maskb[:, hj2 * 16:hj2 * 16 + 1]),
                                sq_b[:, 2 * TQ + jq * 512: 2 * TQ + (jq + 1) * 512],
                                start=False, stop=True))(jq, hj2))
                        for half in range(2):
                            bp = 32 if half == 0 else 64
                            for j4 in range(2):
                                jj = half * 2 + j4
                                pl.add(TE, (lambda jj, j4, hj2, bp, pbank=pbank:
                                            lambda e: e.matmul(
                                    pbank[bp:bp + 1, j4 * 512:(j4 + 1) * 512], onesb[:],
                                    sk_b[:, hj2 * S + jj * 512: hj2 * S + (jj + 1) * 512],
                                    start=True, stop=False))(jj, j4, hj2, bp))
                                pl.add(TE, (lambda jj, j4, bp, pbank=pbank:
                                            lambda e: e.matmul(
                                    pbank[bp:bp + 1, j4 * 512:(j4 + 1) * 512], onesb[:],
                                    skr_b[:, jj * 512:(jj + 1) * 512],
                                    start=False, stop=True))(jj, j4, bp),
                                    inc=sem_t if (hj2 == 1 and half == 1 and j4 == 1)
                                    else None)
                    wtb = pl.n(sem_t)
                    for hj2 in (range(2) if not NO_BOUND else []):
                        hh = 2 * p + hj2
                        pbank = pB if hj2 == 0 else pA
                        b = hj2 * 4
                        pl.add(VE, (lambda b, pbank=pbank: lambda e: e.reduce_max(
                            stat[0:1, b:b + 1], pbank[0:1, 0:1024], axis=AXX))(b),
                            waits=([(sem_t, wtb)] if hj2 == 0 else []), inc=sem_v)
                        pl.add(VE, (lambda b, pbank=pbank: lambda e: e.reduce_max(
                            stat[0:1, b + 1:b + 2], pbank[32:33, 0:1024], axis=AXX))(b),
                            inc=sem_v)
                        pl.add(VE, (lambda b, pbank=pbank: lambda e: e.reduce_max(
                            stat[0:1, b + 2:b + 3], pbank[64:65, 0:1024], axis=AXX))(b),
                            inc=sem_v)
                        pl.add(VE, (lambda b: lambda e: e.tensor_max(
                            stat[0:1, b + 1:b + 2], stat[0:1, b + 1:b + 2],
                            stat[0:1, b + 2:b + 3]))(b), inc=sem_v)
                        pl.add(VE, (lambda b: lambda e: e.tensor_mul(
                            stat[0:1, b + 3:b + 4], stat[0:1, b:b + 1],
                            stat[0:1, b + 1:b + 2]))(b), inc=sem_v)
                        wv_ = pl.n(sem_v)
                        pl.add(SC, (lambda b: lambda e: e.activation(
                            stat[0:1, b:b + 1], stat[0:1, b + 3:b + 4], AF.Ln))(b),
                            waits=[(sem_v, wv_)], inc=sem_c)
                        pl.add(SC, (lambda b: lambda e: e.activation(
                            stat[0:1, b + 1:b + 2], stat[0:1, b:b + 1], AF.Exp,
                            scale=0.5))(b), inc=sem_c)
                        wc_ = pl.n(sem_c)
                        pl.add(TE, (lambda b, hj2: lambda e: e.matmul(
                            pB[:, hj2:hj2 + 1], c_nsq[:], stat[0:1, b + 1:b + 2],
                            start=True, stop=True))(b, hj2),
                            waits=[(sem_c, wc_)], inc=sem_t)
                        wt2 = pl.n(sem_t)
                        pl.add(VE, (lambda hh, hj2: lambda e: e.tensor_copy(
                            bias_c[:, hh:hh + 1], pB[:, hj2:hj2 + 1]))(hh, hj2),
                            waits=[(sem_t, wt2)], inc=sem_v)
                        bias_done_h[hj2] = pl.n(sem_v)
                        if hh in (0, 2, 5, 13, 26, 27, 28, 31):
                            tap("biasrt_h%d" % hh, bias_c[:, hh:hh + 1], (128, 1), F32,
                                [(sem_v, bias_done_h[hj2])])
                    bnd_red_done[p] = pl.n(sem_v)
                    if p == 13:
                        tap("sq13", sq_b[:], (128, 3 * TQ), BF16, [(sem_c, sq_done)])
                        tap("sk13", sk_b[:], (128, 2 * S), BF16, [(sem_c, sq_done)])
                        tap("qp13", qp[:], (128, 3 * TQ), BF16, [(sem_c, sq_done)])
                        tap("kp13", kp[:], (128, 2 * S), BF16, [(sem_c, sq_done)])

                    for hj in range(2):
                        h = 2 * p + hj
                        pOu = pC if h % 2 == 0 else pD
                        pX = pD if h % 2 == 0 else pC

                        bias_done = bias_done_h[1]

                        for t in range(16):
                            psS = pA if t % 2 == 0 else pB
                            w = []
                            if t < 2:
                                w.append((sem_v, bias_done))
                                if hj == 0:
                                    w.append((sem_c, vev_all))
                                if h > 0:
                                    w.append((sem_c, exp_done[(h - 1, 14 + t)]))
                            else:
                                w.append((sem_c, exp_done[(h, t - 2)]))
                                if t == 3 and h > 0:
                                    w.append((sem_v, rrep_done[h - 1]))
                            for jq in range(2):
                                pl.add(TE, (lambda t, jq, hj, psS=psS: lambda e: e.matmul(
                                    psS[:, jq * 512:(jq + 1) * 512],
                                    kp[:, hj * S + t * 128:hj * S + (t + 1) * 128],
                                    qp[:, hj * TQ + jq * 512: hj * TQ + jq * 512 + 512],
                                    start=True, stop=False))(t, jq, hj),
                                    waits=w if jq == 0 else [])
                                kz = krope2 if hj == 0 else krope2b
                                pl.add(TE, (lambda t, jq, kz=kz, psS=psS: lambda e: e.matmul(
                                    psS[:, jq * 512:(jq + 1) * 512],
                                    kz[:, t * 128:(t + 1) * 128],
                                    qp[:, 2 * TQ + jq * 512: 2 * TQ + jq * 512 + 512],
                                    start=False, stop=True))(t, jq),
                                    inc=sem_t if jq == 1 else None)
                            smm_done[(h, t)] = pl.n(sem_t)

                            if t == 0 and tailA_pending:
                                tailA_pending.pop()()
                            if t == 2 and tailB_pending:
                                tailB_pending.pop()()

                            if t > 0:
                                ta = t - 1
                                w = [(sem_c, exp_done[(h, ta)])]
                                if ta == 0:
                                    if h >= 1:
                                        w.append((sem_v, recip_done[h - 1]))
                                for jq in range(2):
                                    pl.add(TE, (lambda ta, jq, hj, pOu=pOu: lambda e: e.matmul(
                                        pOu[:, jq * 512:(jq + 1) * 512],
                                        vp[:, ta * 256 + hj * 128: ta * 256 + (hj + 1) * 128],
                                        attnT[:, ta * TQ + jq * 512: ta * TQ + jq * 512 + 512],
                                        start=(ta == 0), stop=(ta == 15)))(ta, jq, hj),
                                        waits=w if jq == 0 else [],
                                        inc=sem_t if jq == 1 else None)
                                av_done[(h, ta)] = pl.n(sem_t)

                            psS = pA if t % 2 == 0 else pB
                            w = [(sem_t, smm_done[(h, t)])]
                            if h > 0:
                                w.append((sem_v, tree_done[(h - 1, t)]))
                                w.append((sem_t, av_done[(h - 1, t)]))
                            pl.add(SC, (lambda t, h, psS=psS: lambda e: e.activation(
                                attnT[:, t * TQ:(t + 1) * TQ], psS[:], AF.Exp,
                                bias=(0.0 if (NO_BOUND or BLVL < 5) else bias_c[:, h:h + 1]),
                                scale=S192))(t, h),
                                waits=w, inc=sem_c)
                            exp_done[(h, t)] = pl.n(sem_c)

                            if True:
                                acc = P_den[:, 0:TQ] if t % 2 == 0 else P_den2[:]
                                w = [(sem_c, exp_done[(h, t)])]
                                if t <= 1 and h > 0:
                                    w.append((sem_t, dmm_done[h - 1]))
                                if t <= 1:
                                    fn = (lambda t, acc=acc: lambda e: e.tensor_copy(
                                        acc, attnT[:, t * TQ:(t + 1) * TQ]))(t)
                                else:
                                    fn = (lambda t, acc=acc: lambda e: e.tensor_add(
                                        acc, acc, attnT[:, t * TQ:(t + 1) * TQ]))(t)
                                pl.add(VE, fn, waits=w, inc=sem_v, no_selfwait=True)
                            tree_done[(h, t)] = pl.n(sem_v)

                        ta = 15
                        w = [(sem_c, exp_done[(h, ta)])]
                        for jq in range(2):
                            pl.add(TE, (lambda ta, jq, hj, pOu=pOu: lambda e: e.matmul(
                                pOu[:, jq * 512:(jq + 1) * 512],
                                vp[:, ta * 256 + hj * 128: ta * 256 + (hj + 1) * 128],
                                attnT[:, ta * TQ + jq * 512: ta * TQ + jq * 512 + 512],
                                start=False, stop=True))(ta, jq, hj),
                                waits=w if jq == 0 else [],
                                inc=sem_t if jq == 1 else None)
                        av_done[(h, ta)] = pl.n(sem_t)

                        tailA_pending.append(
                            (lambda h=h, pX=pX: emit_tailA(h, pX)))
                        tailB_pending.append(
                            (lambda h=h, pOu=pOu: emit_tailB(h, pOu)))

                if tailA_pending:
                    tailA_pending.pop()()
                if tailB_pending:
                    tailB_pending.pop()()
                tap("osum", o_sumT[:], (128, TQ), F32, [(sem_v, norm_done[2 * N_PAIRS - 1])])
                tap("biasall", bias_c[:], (128, NH), F32,
                    [(sem_v, norm_done[2 * N_PAIRS - 1])])
                pl.add(VE, lambda e: e.tensor_scalar_add(o_sumT[:], o_sumT[:], bvsum[:]),
                       waits=[(sem_v, norm_done[2 * N_PAIRS - 1]), (sem_v, consts_v),
                              (sem_o, tapw.get("osum", 0))], inc=sem_v)
                p3_c = pl.n(sem_c)
                p3_t = pl.n(sem_t)
                p3_v = pl.n(sem_v)

        if SKIP_P4:
            pl.add(SY, lambda e: e.dma_start(out=out_d[0:128, 0:TQ], in_=o_sumT[:]),
                   waits=[(sem_c, p3_c)], inc=sem_o, inc_k=16)
            pl.add(SY, lambda e: e.nop(),
                   waits=[(s_st[0], pl.n(s_st[0])), (s_st[1], pl.n(s_st[1])),
                          (sem_o, pl.n(sem_o))])
            with nc.Block() as block:
                @block.gpsimd
                def _(e):
                    pl.run(G, e)

                @block.sync
                def _(e):
                    pl.run(SY, e)

                @block.tensor
                def _(e):
                    pl.run(TE, e)

                @block.scalar
                def _(e):
                    pl.run(SC, e)

                @block.vector
                def _(e):
                    pl.run(VE, e)
            return nc

        with ExitStack() as sc4:
            sb4 = lambda n, sh, dt: sc4.enter_context(nc.sbuf_tensor(n, list(sh), dt))
            Zs = sb4("Zs", (128, 4 * H), F32)
            Bi = sb4("Bi", (128, H // 2), I32)
            eq = sb4("eq", (128, H // 2), F32)
            xqneg = sb4("xqneg", (128, H), BF16)
            xqT = sb4("xqT", (128, 4 * 32 * 128), BF16)
            cbb = sb4("cbb", (128, 3 * 8 * 512), BF16)
            woT_f = sb4("woT_f", (128, H), F32)
            bo_rep = sb4("bo_rep", (128, H), BF16)
            scb_rep = sb4("scb_rep", (128, H), F32)
            st4 = sb4("st4", (128, 128), F32)

            pXq = sc4.enter_context(nc.psum_tensor("pXq", [128, 1024], BF16))
            pE = sc4.enter_context(nc.psum_tensor("pE", [128, 512], F32))

            cbd = cbT.rearrange("(c p) m -> p c m", p=128)
            cb4 = cbb[:].rearrange("p (b c m) -> p b c m", b=3, c=8)
            xqT4 = xqT[:].rearrange("p (s c m) -> p s c m", s=4, c=32)
            xo3 = x_own.rearrange("(g p) m -> g p m", p=128)
            out3 = out_d.rearrange("(g p) m -> g p m", p=128)

            pl.add(G, lambda e: e.dma_start(out=bo_rep[:], in_=bo_mat),
                   waits=[(sem_t, p3_t), (sem_c, p3_c), (sem_v, p3_v)],
                   inc=g_misc, inc_k=16)
            w_bo = (g_misc, pl.n(g_misc))
            pl.add(SY, lambda e: e.dma_start(out=scb_rep[:], in_=scb_mat),
                   waits=[(sem_t, p3_t), (sem_c, p3_c), (sem_v, p3_v)],
                   inc=s_woT, inc_k=16)
            w_scb = pl.n(s_woT)
            pl.add(SY, lambda e: e.dma_start(out=woT_f[:], in_=woT),
                   inc=s_woT, inc_k=16)
            w_woT = pl.n(s_woT)
            pl.add(VE, lambda e: e.tensor_scalar_mul(
                scb_rep[:], scb_rep[:], 1.0 / 127.0),
                waits=[(s_woT, w_scb)], inc=sem_v)
            rep_done = pl.n(sem_v)

            ln2ap = {}
            st_done = {}
            cbu_done = {}
            slab_done = {}
            opev = {}
            opev_last = 0
            trev_last = 0
            tr_done = {}
            ffev = {}
            ffn_pair_done = {}
            xq_v_e = {}
            chain_guard = {-1: 0}

            def zslice(e):
                return Zs[:, (e % 4) * H:(e % 4 + 1) * H]

            def slab_load(gu):
                bufu = gu % 3
                u = gu % 32
                w = ([(sem_t, cbu_done[gu - 3])] if gu >= 3 else
                     [(sem_t, p3_t), (sem_c, p3_c), (sem_v, p3_v)])
                pl.add(G, (lambda u, bufu: lambda e: e.dma_start(
                    out=cb4[:, bufu, :, :],
                    in_=cbd[:, (u % 4) * 8:(u % 4 + 1) * 8,
                            (u // 4) * 512:(u // 4 + 1) * 512]))(u, bufu),
                    waits=w, inc=g_cb[bufu], inc_k=16)
                slab_done[gu] = (g_cb[bufu], pl.n(g_cb[bufu]))

            def emit_prep(e):
                s = e % 2
                zsl = zslice(e)
                w = [st_done.get((e - 4), (s_st[s], 0)),
                     (sem_c, ln2ap.get(e - 4, 0))]
                if e < 2:
                    w += [(sem_t, p3_t), (sem_c, p3_c), (sem_v, p3_v)]
                pl.add(SY, (lambda e, zsl=zsl: lambda ee: ee.dma_start(
                    out=zsl, in_=xo3[e, :, :]))(e),
                    waits=w, inc=s_xz[s], inc_k=16)
                xl = (s_xz[s], pl.n(s_xz[s]))
                pl.add(VE, (lambda zsl=zsl: lambda ee: ee.tensor_add(
                    zsl, zsl, bo_rep[:]))(),
                    waits=[xl] + ([w_bo] if e == 0 else []), inc=sem_v)

            def emit_opmm(e):
                nonlocal opev_last
                zsl = zslice(e)
                C = (e % 4) * 32
                for j in range(8):
                    psb = pB[:, (j % 2) * 512:(j % 2 + 1) * 512]
                    w = []
                    if e == 0 and j == 0:
                        w = [(sem_c, p3_c), (s_woT, w_woT), (sem_v, rep_done)]
                    else:
                        w.append((sem_v, opev.get((e, j - 2), opev_last)))
                    pl.add(TE, (lambda e, j, psb=psb: lambda ee: ee.matmul(
                        psb, o_sumT[:, e * 128:(e + 1) * 128],
                        woT_f[:, j * 512:(j + 1) * 512],
                        start=True, stop=True))(e, j), waits=w, inc=sem_t)
                    wt_ = pl.n(sem_t)
                    pl.add(VE, (lambda j, C, zsl=zsl, psb=psb: lambda ee:
                                ee.scalar_tensor_tensor(
                        zsl[:, j * 512:(j + 1) * 512], psb, 1.0,
                        zsl[:, j * 512:(j + 1) * 512], AL.bypass, AL.add,
                        accum_out=st4[:, C + 16 + j:C + 17 + j]))(j, C),
                        waits=[(sem_t, wt_)], inc=sem_v)
                    opev[(e, j)] = pl.n(sem_v)
                    opev_last = opev[(e, j)]

            def make_chain(e):
                zsl = zslice(e)
                C = (e % 4) * 32
                items = []
                A = items.append
                A(lambda: pl.add(VE, (lambda C=C: lambda ee: ee.reduce_sum(
                    st4[:, C:C + 1], st4[:, C + 16:C + 24], axis=AXX,
                    negate=True))(), inc=sem_v))
                A(lambda: pl.add(VE, (lambda C=C: lambda ee: ee.tensor_scalar_mul(
                    st4[:, C:C + 1], st4[:, C:C + 1], 1.0 / H))(), inc=sem_v))
                def _c():
                    wst = [(sem_v, pl.n(sem_v))]
                    pl.add(SC, (lambda zsl=zsl, C=C: lambda ee: ee.activation(
                        zsl, zsl, AF.Identity, bias=st4[:, C:C + 1]))(),
                        waits=wst, inc=sem_c)
                A(_c)
                def _mc():
                    wc_ = pl.n(sem_c)
                    pl.add(VE, (lambda zsl=zsl, C=C: lambda ee: ee.reduce_max(
                        st4[:, C + 1:C + 2], zsl, axis=AXX,
                        apply_absolute_value=True))(),
                        waits=[(sem_c, wc_)], inc=sem_v)
                A(_mc)
                ssq_c = [0]
                def _ssq():
                    pl.add(SC, (lambda zsl=zsl, C=C: lambda ee: ee.activation(
                        xqneg[:], zsl, AF.Square,
                        accum_out=st4[:, C + 2:C + 3]))(),
                        waits=[(sem_t, chain_guard.get(e - 1, 0))], inc=sem_c)
                    ssq_c[0] = pl.n(sem_c)
                A(_ssq)
                def _var():
                    wcq = pl.n(sem_c)
                    pl.add(VE, (lambda C=C: lambda ee: ee.tensor_scalar(
                        st4[:, C + 3:C + 4], st4[:, C + 2:C + 3], 1.0 / H, EPS,
                        AL.mult, AL.add))(), waits=[(sem_c, wcq)], inc=sem_v)
                A(_var)
                def _ln():
                    wv_ = pl.n(sem_v)
                    pl.add(SC, (lambda C=C: lambda ee: ee.activation(
                        st4[:, C + 4:C + 5], st4[:, C + 3:C + 4], AF.Ln))(),
                        waits=[(sem_v, wv_)], inc=sem_c)
                    pl.add(SC, (lambda C=C: lambda ee: ee.activation(
                        st4[:, C + 5:C + 6], st4[:, C + 4:C + 5], AF.Exp,
                        scale=-0.5))(), inc=sem_c)
                A(_ln)
                def _sx():
                    wc_ = pl.n(sem_c)
                    pl.add(VE, (lambda C=C: lambda ee: ee.reciprocal(
                        st4[:, C + 7:C + 8], st4[:, C + 1:C + 2]))(),
                        waits=[(sem_c, wc_)], inc=sem_v)
                    pl.add(VE, (lambda C=C: lambda ee: ee.tensor_scalar_mul(
                        st4[:, C + 7:C + 8], st4[:, C + 7:C + 8], 127.0))(),
                        inc=sem_v)
                    pl.add(VE, (lambda C=C: lambda ee: ee.tensor_mul(
                        st4[:, C + 8:C + 9], st4[:, C + 1:C + 2],
                        st4[:, C + 5:C + 6]))(), inc=sem_v)
                    pl.add(VE, (lambda C=C: lambda ee: ee.tensor_scalar_mul(
                        st4[:, C + 8:C + 9], st4[:, C + 8:C + 9], 1.0 / 127.0))(),
                        inc=sem_v)
                    pl.add(VE, (lambda C=C: lambda ee: ee.tensor_scalar_mul(
                        st4[:, C + 9:C + 10], st4[:, C + 8:C + 9], -1.0))(),
                        inc=sem_v)
                A(_sx)
                A(lambda: pl.add(VE, (lambda zsl=zsl, C=C: lambda ee:
                                      ee.tensor_scalar_mul(
                    zsl, zsl, st4[:, C + 7:C + 8]))(),
                    waits=[(sem_c, ssq_c[0])], inc=sem_v))
                HH = H // 2
                for hf in range(2):
                    a2 = zsl[:, hf * HH:(hf + 1) * HH]
                    xq2 = xqneg[:, hf * HH:(hf + 1) * HH]
                    def _i32(a2=a2):
                        pl.add(SC, (lambda a2=a2: lambda ee: ee.copy(Bi[:], a2))(),
                               waits=[(sem_v, pl.n(sem_v))], inc=sem_c)
                    A(_i32)
                    def _d1(a2=a2):
                        wc_ = pl.n(sem_c)
                        pl.add(VE, (lambda a2=a2: lambda ee:
                                    ee.scalar_tensor_tensor(
                            eq[:], a2, 0.5, Bi[:], AL.subtract, AL.is_ge))(),
                            waits=[(sem_c, wc_)], inc=sem_v)
                    A(_d1)
                    A(lambda: pl.add(VE, (lambda ee: ee.tensor_add(
                        eq[:], eq[:], Bi[:])), inc=sem_v))
                    A((lambda a2=a2, xq2=xq2: lambda: pl.add(VE,
                        (lambda a2=a2, xq2=xq2: lambda ee: ee.scalar_tensor_tensor(
                            xq2, a2, 0.5, Bi[:], AL.add, AL.is_le))(), inc=sem_v))(a2, xq2))
                    A((lambda xq2=xq2: lambda: pl.add(VE,
                        (lambda xq2=xq2: lambda ee: ee.tensor_sub(
                            xq2, xq2, eq[:]))(), inc=sem_v))(xq2))
                def _fin():
                    xq_v_e[e] = pl.n(sem_v)
                A(_fin)
                return items

            def emit_transposes(e):
                nonlocal trev_last
                s = e % 4
                for hg in range(4):
                    for k in range(8):
                        hc = hg * 8 + k
                        w = []
                        if k == 0:
                            w = [(sem_v, xq_v_e[e]), (sem_c, trev_last)]
                            if hg == 0:
                                w.append((sem_t, ffn_pair_done.get(e // 2 - 2, 0)))
                        pl.add(TE, (lambda hc, k: lambda ee: ee.transpose(
                            pXq[:, k * 128:(k + 1) * 128],
                            xqneg[:, hc * 128:(hc + 1) * 128], c_eyeb[:]))(hc, k),
                            waits=w, inc=sem_t if k == 7 else None)
                    wt_ = pl.n(sem_t)
                    pl.add(SC, (lambda hg, s: lambda ee: ee.copy(
                        xqT4[:, s, hg * 8:(hg + 1) * 8, :],
                        pXq[:].rearrange("p (c m) -> p c m", c=8)))(hg, s),
                        waits=[(sem_t, wt_)], inc=sem_c)
                    trev_last = pl.n(sem_c)
                tr_done[e] = trev_last
                chain_guard[e] = pl.n(sem_t)

            def emit_sweep(q, hook=None, hook2=None):
                e0, e1 = 2 * q, 2 * q + 1
                zs = [zslice(e0), zslice(e1)]
                Cs = [(e0 % 4) * 32, (e1 % 4) * 32]
                ss = [e0 % 4, e1 % 4]
                for oj in range(8):
                    if hook is not None:
                        hook(oj)
                    if oj % 2 == 0:
                        accs = [pA[:, 0:512], pA[:, 512:1024]]
                    else:
                        accs = [pC[:, 512:1024], pE[:, 0:512]]
                    for uu in range(4):
                        gu = q * 32 + oj * 4 + uu
                        bufu = gu % 3
                        for ei in range(2):
                            for k in range(8):
                                hc = uu * 8 + k
                                w = []
                                if uu == 0 and k == 0 and ei == 0:
                                    if oj == 0:
                                        w.append((sem_c, tr_done[e1]))
                                    pk = (q, oj - 2) if oj >= 2 else (q - 1, oj + 6)
                                    if pk in ffev:
                                        w.append((sem_v, ffev[pk]))
                                    w.append(slab_done[gu])
                                elif k == 0 and ei == 0:
                                    w.append(slab_done[gu])
                                pl.add(TE, (lambda hc, k, bufu, acc=accs[ei],
                                            sN=ss[ei], st_=(uu == 0 and k == 0),
                                            sp_=(uu == 3 and k == 7):
                                            lambda ee: ee.matmul(
                                    acc, xqT4[:, sN, hc, :], cb4[:, bufu, k, :],
                                    start=st_, stop=sp_))(hc, k, bufu),
                                    waits=w,
                                    inc=sem_t if (ei == 1 and k == 7) else None)
                        cbu_done[gu] = pl.n(sem_t)
                        if gu + 3 < 128:
                            slab_load(gu + 3)
                    wt_ = pl.n(sem_t)
                    for ei in range(2):
                        C = Cs[ei]
                        zsl = zs[ei]
                        acc = accs[ei]
                        pl.add(VE, (lambda oj, C, acc=acc: lambda ee:
                                    ee.scalar_tensor_tensor(
                            pC[:, 0:512], acc, st4[:, C + 9:C + 10],
                            scb_rep[:, oj * 512:(oj + 1) * 512],
                            AL.mult, AL.mult))(oj, C),
                            waits=[(sem_t, wt_)] if ei == 0 else [], inc=sem_v)
                        pl.add(VE, (lambda oj, C, zsl=zsl: lambda ee:
                                    ee.scalar_tensor_tensor(
                            zsl[:, oj * 512:(oj + 1) * 512],
                            zsl[:, oj * 512:(oj + 1) * 512],
                            st4[:, C + 8:C + 9], pC[:, 0:512], AL.mult, AL.add,
                            accum_out=st4[:, C + 24 + oj:C + 25 + oj]))(oj, C),
                            inc=sem_v)
                    ffev[(q, oj)] = pl.n(sem_v)
                    if hook2 is not None:
                        hook2(oj)
                ffn_pair_done[q] = pl.n(sem_t)

            def emit_back(e):
                zsl = zslice(e)
                C = (e % 4) * 32
                pl.add(VE, (lambda C=C: lambda ee: ee.reduce_sum(
                    st4[:, C + 10:C + 11], st4[:, C + 24:C + 32], axis=AXX,
                    negate=True))(),
                    waits=[(sem_v, ffev[(e // 2, 7)])], inc=sem_v)
                pl.add(VE, (lambda C=C: lambda ee: ee.tensor_scalar_mul(
                    st4[:, C + 10:C + 11], st4[:, C + 10:C + 11], 1.0 / H))(),
                    inc=sem_v)
                pl.add(SC, (lambda zsl=zsl, C=C: lambda ee: ee.activation(
                    xqneg[:], zsl, AF.Square,
                    accum_out=st4[:, C + 12:C + 13]))(),
                    waits=[(sem_t, chain_guard.get(e + 3, pl.n(sem_t))),
                           (sem_v, ffev[(e // 2, 7)])], inc=sem_c)
                wc0 = pl.n(sem_c)
                pl.add(VE, (lambda C=C: lambda ee: ee.tensor_mul(
                    st4[:, C + 13:C + 14], st4[:, C + 10:C + 11],
                    st4[:, C + 10:C + 11]))(), inc=sem_v)
                pl.add(VE, (lambda C=C: lambda ee: ee.scalar_tensor_tensor(
                    st4[:, C + 13:C + 14], st4[:, C + 12:C + 13], 1.0 / H,
                    st4[:, C + 13:C + 14], AL.mult, AL.subtract))(),
                    waits=[(sem_c, wc0)], inc=sem_v)
                pl.add(VE, (lambda C=C: lambda ee: ee.tensor_scalar(
                    st4[:, C + 13:C + 14], st4[:, C + 13:C + 14], 1.0, EPS,
                    AL.mult, AL.add))(), inc=sem_v)
                wv_ = pl.n(sem_v)
                pl.add(SC, (lambda C=C: lambda ee: ee.activation(
                    st4[:, C + 14:C + 15], st4[:, C + 13:C + 14], AF.Ln))(),
                    waits=[(sem_v, wv_)], inc=sem_c)
                pl.add(SC, (lambda C=C: lambda ee: ee.activation(
                    st4[:, C + 15:C + 16], st4[:, C + 14:C + 15], AF.Exp,
                    scale=-0.5))(), inc=sem_c)
                wc_ = pl.n(sem_c)
                pl.add(VE, (lambda C=C: lambda ee: ee.tensor_mul(
                    st4[:, C + 14:C + 15], st4[:, C + 15:C + 16],
                    st4[:, C + 15:C + 16]))(), waits=[(sem_c, wc_)], inc=sem_v)
                pl.add(VE, (lambda C=C: lambda ee: ee.tensor_mul(
                    st4[:, C + 14:C + 15], st4[:, C + 14:C + 15],
                    st4[:, C + 13:C + 14]))(), inc=sem_v)
                pl.add(VE, (lambda C=C: lambda ee: ee.tensor_scalar(
                    st4[:, C + 14:C + 15], st4[:, C + 14:C + 15], -0.5, 1.5,
                    AL.mult, AL.add))(), inc=sem_v)
                pl.add(VE, (lambda C=C: lambda ee: ee.tensor_mul(
                    st4[:, C + 15:C + 16], st4[:, C + 15:C + 16],
                    st4[:, C + 14:C + 15]))(), inc=sem_v)
                pl.add(VE, (lambda C=C: lambda ee: ee.tensor_mul(
                    st4[:, C + 11:C + 12], st4[:, C + 10:C + 11],
                    st4[:, C + 15:C + 16]))(), inc=sem_v)
                wv_ = pl.n(sem_v)
                pl.add(SC, (lambda zsl=zsl, C=C: lambda ee: ee.activation(
                    zsl, zsl, AF.Identity, bias=st4[:, C + 11:C + 12],
                    scale=st4[:, C + 15:C + 16]))(),
                    waits=[(sem_v, wv_)], inc=sem_c)
                ln2ap[e] = pl.n(sem_c)
                pl.add(SY, (lambda e, zsl=zsl: lambda ee: ee.dma_start(
                    out=out3[e, :, :], in_=zsl))(e),
                    waits=[(sem_c, ln2ap[e])], inc=s_st[e % 2], inc_k=16)
                st_done[e] = (s_st[e % 2], pl.n(s_st[e % 2]))

            for gu0 in range(3):
                slab_load(gu0)
            emit_prep(0)
            emit_prep(1)
            emit_opmm(0)
            for it in make_chain(0):
                it()
            emit_transposes(0)
            emit_opmm(1)
            for it in make_chain(1):
                it()
            emit_transposes(1)

            for q in range(4):
                if q < 3:
                    chain_items = []

                    def hookA(oj, q=q, ci=chain_items):
                        e2, e3 = 2 * q + 2, 2 * q + 3
                        if oj == 1:
                            ci.extend(make_chain(e2))
                            for _ in range((len(ci) + 2) // 3):
                                if ci:
                                    ci.pop(0)()
                        elif oj == 2:
                            for _ in range((len(ci) + 1) // 2):
                                if ci:
                                    ci.pop(0)()
                        elif oj == 3:
                            while ci:
                                ci.pop(0)()
                        elif oj == 4:
                            emit_transposes(e2)
                        elif oj == 5:
                            ci.extend(make_chain(e3))
                            for _ in range((len(ci) + 2) // 3):
                                if ci:
                                    ci.pop(0)()
                        elif oj == 6:
                            for _ in range((len(ci) + 1) // 2):
                                if ci:
                                    ci.pop(0)()
                        elif oj == 7:
                            while ci:
                                ci.pop(0)()
                            emit_transposes(e3)

                    def hookB(oj, q=q):
                        e2, e3 = 2 * q + 2, 2 * q + 3
                        if oj == 0:
                            if q > 0:
                                emit_back(2 * q - 2)
                            emit_prep(e2)
                            emit_opmm(e2)
                        elif oj == 1:
                            if q > 0:
                                emit_back(2 * q - 1)
                        elif oj == 4:
                            emit_prep(e3)
                            emit_opmm(e3)
                    emit_sweep(q, hookA, hookB)
                else:
                    def hook3B(oj):
                        if oj == 0:
                            emit_back(4)
                        elif oj == 1:
                            emit_back(5)
                    emit_sweep(q, None, hook3B)
            emit_back(6)
            emit_back(7)

            pl.add(SY, lambda e: e.nop(),
                   waits=[(s_st[0], pl.n(s_st[0])), (s_st[1], pl.n(s_st[1])),
                          (sem_o, pl.n(sem_o))])

            with nc.Block() as block:
                @block.gpsimd
                def _(e):
                    pl.run(G, e)

                @block.sync
                def _(e):
                    pl.run(SY, e)

                @block.tensor
                def _(e):
                    pl.run(TE, e)

                @block.scalar
                def _(e):
                    pl.run(SC, e)

                @block.vector
                def _(e):
                    pl.run(VE, e)
            return nc


_NC_CACHE = None


def _get_nc():
    global _NC_CACHE
    if _NC_CACHE is None:
        _NC_CACHE = build()
    return _NC_CACHE


def _shards(inputs):
    f32 = np.float32
    x = np.asarray(inputs["L0_attn_hidden_size"], f32)
    Wq_c = np.asarray(inputs["L0_attn_Wq_c"], f32)
    bq_c = np.asarray(inputs["L0_attn_bq_c"], f32)
    Wq_up = np.asarray(inputs["L0_attn_Wq_up"], f32)
    bq_up = np.asarray(inputs["L0_attn_bq_up"], f32)
    Wkv_c = np.asarray(inputs["L0_attn_Wkv_c"], f32)
    bkv_c = np.asarray(inputs["L0_attn_bkv_c"], f32)
    Wkv_up = np.asarray(inputs["L0_attn_Wkv_up"], f32)
    bkv_up = np.asarray(inputs["L0_attn_bkv_up"], f32)
    Wo = np.asarray(inputs["L0_attn_Wo"], f32)
    bo = np.asarray(inputs["L0_attn_bo"], f32)
    CB = np.asarray(inputs["L0_ffn_CB"], np.int8)
    SCB = np.asarray(inputs["L0_ffn_SCB"], f32)

    perm = []
    for p in range(NP):
        h1, h2 = 2 * p, 2 * p + 1
        perm += list(range(h1 * QH, h1 * QH + NOPE))
        perm += list(range(h2 * QH, h2 * QH + NOPE))
        perm += list(range(h1 * QH + NOPE, h1 * QH + QH))
        perm += list(range(h2 * QH + NOPE, h2 * QH + QH))
    perm = np.asarray(perm)
    kidx = np.concatenate([np.arange(h * 256, h * 256 + 128) for h in range(NH)])
    vidx = np.concatenate([np.arange(h * 256 + 128, h * 256 + 256) for h in range(NH)])

    common = dict(
        wqcT=np.ascontiguousarray(Wq_c.T),
        wqupT=np.ascontiguousarray(Wq_up.T[:, perm]),
        wkvcT=np.ascontiguousarray(Wkv_c.T),
        wkT=np.ascontiguousarray(Wkv_up.T[:, kidx]),
        wvT=np.ascontiguousarray(Wkv_up.T[:, vidx]),
        woT=np.ascontiguousarray(Wo.T),
        cbT=np.ascontiguousarray(CB.T),
        bqc_m=np.ascontiguousarray(bq_c.reshape(12, 128).T),
        bqup_m=np.ascontiguousarray(bq_up[perm].reshape(48, 128).T),
        bkvc_a=np.ascontiguousarray(bkv_c[:512].reshape(4, 128).T),
        bkvc_b=np.ascontiguousarray(bkv_c[512:].reshape(1, 64).T),
        bk_m=np.ascontiguousarray(bkv_up[kidx].reshape(32, 128).T),
        bv_m=np.ascontiguousarray(bkv_up[vidx].reshape(32, 128).T),
        bo_mat=np.ascontiguousarray(np.broadcast_to(bo.reshape(1, H), (128, H))),
        scb_mat=np.ascontiguousarray(np.broadcast_to(SCB.reshape(1, H), (128, H))),
        ones_c=np.ones((128, 1), f32),
        ones_r=np.ones((1, 128), f32),
        nsqrt_r=np.full((1, 128), -1.0 / math.sqrt(192.0), f32),
        mask0=np.concatenate([np.ones((64, 1), f32), np.zeros((64, 1), f32)]),
        mask1=np.concatenate([np.zeros((64, 1), f32), np.ones((64, 1), f32)]),
        eye_f=np.eye(128, dtype=f32),
    )
    maps = []
    for c in range(8):
        b, sh = c // 2, c % 2
        xT = np.ascontiguousarray(x[b].T)
        m = dict(common)
        m["xT_k0"] = np.ascontiguousarray(xT[:, :1024])
        m["xT_k1"] = np.ascontiguousarray(xT[:, 1024:])
        m["xT_q"] = m["xT_k0"] if sh == 0 else m["xT_k1"]
        m["x_own"] = np.ascontiguousarray(x[b, sh * 1024:(sh + 1) * 1024])
        maps.append(m)
    return maps


def kernel(**inputs):
    maps = _shards(inputs)
    nc = _get_nc()
    res = run_bass_kernel_spmd(nc, maps, core_ids=list(range(8)))
    out = np.empty((4, 2048, 4096), np.float32)
    for c in range(8):
        b, sh = c // 2, c % 2
        out[b, sh * 1024:(sh + 1) * 1024] = res.results[c]["out"]
    return out


if __name__ == "__main__":
    build()
    print("built ok")
